# version 09
# baseline (speedup 1.0000x reference)
import numpy as np
from contextlib import ExitStack
import concourse.bass as bass
import concourse.mybir as mybir
from concourse.bass_utils import run_bass_kernel_spmd

F32 = mybir.dt.float32
AF = mybir.ActivationFunctionType
AL = mybir.AluOpType

NCORES = 8
NSEQ, T, D = 2, 2048, 1024
NQB, QBT = 4, 1024
E, H = 8, 4096
NTH, THT = 4, 768

ENGINES = ("sync", "tensor", "vector", "scalar", "gpsimd")


class Prog:
    def __init__(self):
        self.steps = []
        self.cnt = {}
        self._waited = {e: {} for e in ENGINES}

    def sem(self, name):
        self.cnt.setdefault(name, 0)

    def op(self, eng, build, waits=(), inc=None, n=1):
        wl = []
        for s, v in waits:
            val = self.cnt[s] if v is None else v
            if val <= 0:
                continue
            if self._waited[eng].get(s, -1) >= val:
                continue
            self._waited[eng][s] = val
            wl.append((s, val))
        if inc is not None:
            self.cnt[inc] += n

        def f(e, S, wl=wl, build=build, inc=inc, n=n):
            for s, v in wl:
                e.wait_ge(S[s], v)
            ins = build(e, S)
            if inc is not None:
                assert ins is not None, "inc on op that returned no instruction"
                ins.then_inc(S[inc], n)
        self.steps.append((eng, f))

    def cur(self, sem):
        return (sem, self.cnt[sem])


def build_nc():
    nc = bass.Bass("TRN2", target_bir_lowering=False)
    P = Prog()

    def din(name, shape):
        return nc.dram_tensor(name, shape, F32, kind="ExternalInput").ap()
    xin = din("xin", [D, NQB * QBT])
    wq, wk, wv, wo = din("wq", [D, D]), din("wk", [D, 256]), din("wv", [D, 256]), din("wo", [D, D])
    bqd, bkvd, bod = din("bqd", [128, 8]), din("bkvd", [128, 4]), din("bod", [128, 8])
    gwd, gbd = din("gwd", [128, 8, 8]), din("gbd", [8, 1])
    b1d, b2d = din("b1d", [128, 8, 32]), din("b2d", [128, 8, 8])
    w1d, w2d = din("w1d", [E, D, H]), din("w2d", [E, H, D])
    cosd, sind = din("cosd", [128, T]), din("sind", [128, T])
    mskd = nc.dram_tensor("mskd", [128, 4, 512], mybir.dt.bfloat16, kind="ExternalInput").ap()
    idnd, selmd = din("idnd", [128, 128]), din("selmd", [128, 128])
    tkdd = din("tkdd", [128, 32])
    BF16 = mybir.dt.bfloat16
    U16 = mybir.dt.uint16
    U32 = mybir.dt.uint32
    RW = 2080
    CAP = 1536
    out = nc.dram_tensor("out", [NQB * QBT, D], F32, kind="ExternalOutput").ap()
    w1hd = nc.dram_tensor("w1hd", [E, 8, 128, 8, 512], BF16, kind="Internal").ap()
    w1ld = nc.dram_tensor("w1ld", [E, 8, 128, 8, 512], BF16, kind="Internal").ap()
    w2hd = nc.dram_tensor("w2hd", [E, 8, 128, 4, 1024], BF16, kind="Internal").ap()
    w2ld = nc.dram_tensor("w2ld", [E, 8, 128, 4, 1024], BF16, kind="Internal").ap()
    xbuf = nc.dram_tensor("xbuf", [E, CAP, RW], U16, kind="Internal").ap()

    xin_r = xin.rearrange("(kc p) n -> p kc n", p=128)
    xbuf_z = xbuf.rearrange("e c w -> (e c w)")
    xbuf_g = xbuf.rearrange("e (rc p) w -> e p rc w", p=128)
    wq_r = wq.rearrange("(kc p) m -> p kc m", p=128)
    wk_r = wk.rearrange("(kc p) m -> p kc m", p=128)
    wv_r = wv.rearrange("(kc p) m -> p kc m", p=128)
    wo_r = wo.rearrange("(kc p) m -> p kc m", p=128)
    w1_r = w1d.rearrange("e (kc p) m -> e p kc m", p=128)
    w2_r = w2d.rearrange("e (mc p) m -> e p mc m", p=128)

    with ExitStack() as top:
        sb = lambda name, shape: top.enter_context(nc.sbuf_tensor(name, shape, F32))
        mskb = top.enter_context(nc.sbuf_tensor("mskb", [128, 4, 512], mybir.dt.bfloat16))
        idn = sb("idn", [128, 128])
        selm = sb("selm", [128, 128])
        onc = sb("onc", [128, 1])
        onr = sb("onr", [1, 128])
        epsb = sb("epsb", [128, 1])
        bq_sb = sb("bq_sb", [128, 8])
        bkv_sb = sb("bkv_sb", [128, 4])
        bo_sb = sb("bo_sb", [128, 8])
        gw_sb = sb("gw_sb", [128, 8, 8])
        gb_sb = sb("gb_sb", [8, 1])
        b1_sb = sb("b1_sb", [128, 8, 32])
        b2_sb = sb("b2_sb", [128, 8, 8])

        S = {}
        semnames = ["sC", "sX", "sWQ", "sWKV", "sWO", "sCS", "sW1", "sW2", "sXN",
                    "sSP", "sXO", "sGB", "sGR",
                    "cSQ", "cST", "cBC", "cXN", "cQP", "cQE", "cKV", "cKE",
                    "cVT", "cVE", "cKR", "cSC", "cMK", "cEX", "cAV", "cAE",
                    "cSE", "cRC", "cNX", "cNM", "cWO", "cG", "cGT", "phA",
                    "cH", "cHE", "cY", "cYE", "cSTG", "cVv", "cRV", "cSV", "cIN", "cWS", "cHL", "sWB",
                    "sZB", "cXT", "cXE", "cRT", "cRH", "cRL", "sAD", "sGA", "cYT", "cYS", "cXF", "cXG"] + [f"sA{r}" for r in range(16)]
        for name in semnames:
            S[name] = top.enter_context(nc.semaphore(name))
            P.sem(name)

        def _setup_regs(e, S):
            S["bndx"] = []
            for i in range(E):
                r = e.alloc_register(f"bnd{i}")
                e.reg_mov(r, (i + 1) * CAP - 1)
                S["bndx"].append(r)
            r = e.alloc_register("bndo")
            e.reg_mov(r, NQB * QBT - 1)
            S["bndo"] = r
        P.op("gpsimd", lambda e, S: _setup_regs(e, S) or None)

        def dma(qeng, dst, src, inc, waits=()):
            P.op(qeng, lambda e, S, d=dst, s=src: e.dma_start(out=d, in_=s),
                 waits=waits, inc=inc, n=16)

        for dst, src in ((mskb[:], mskd), (idn[:], idnd), (selm[:], selmd),
                         (bq_sb[:], bqd), (bkv_sb[:], bkvd), (bo_sb[:], bod),
                         (gw_sb[:], gwd), (gb_sb[:], gbd), (b1_sb[:], b1d),
                         (b2_sb[:], b2d)):
            dma("sync", dst, src, "sC")
        P.op("vector", lambda e, S: e.memset(onc[:], 1.0), inc="cIN")
        P.op("vector", lambda e, S: e.memset(epsb[:], 1e-5), inc="cIN")
        P.op("vector", lambda e, S: e.memset(onr[:], 1.0), inc="cIN")
        CW = [P.cur("sC"), P.cur("cIN")]

        att = ExitStack()
        asb = lambda name, shape: att.enter_context(nc.sbuf_tensor(name, shape, F32))
        xq = asb("xq", [128, 8, QBT])
        qa = asb("qa", [128, 8, QBT])
        ksb = asb("ksb", [128, 2, T])
        krep = asb("krep", [128, 2, T])
        vsb = asb("vsb", [128, 2, 2, 512])
        vtm = asb("vtm", [128, 4, 16, 66])
        expb = asb("expb", [128, 2, 4, 512])
        csb = asb("csb", [128, 2, 512])
        wqs = asb("wqs", [128, 2, 8, 128])
        wkvs = asb("wkvs", [128, 2, 4, 128])
        tmp3 = asb("tmp3", [128, 3, 512])
        xnt = asb("xnt", [128, 2, 512])
        meanb = asb("meanb", [128, 512])
        rstdb = asb("rstdb", [128, 512])
        sume = asb("sume", [128, QBT])
        reci = asb("reci", [128, 1, QBT])
        recb = asb("recb", [128, 512])
        lgq = asb("lgq", [8, QBT])
        gsct = asb("gsct", [128, 64])
        gsc2 = asb("gsc2", [128, 2, 64])
        gsm = asb("gsm", [128, 4, 8])
        xts = asb("xts", [128, 2, 1024])
        rsg = att.enter_context(nc.sbuf_tensor("rsg", [128, 2, 2080], mybir.dt.uint16))
        sidxT = asb("sidxT", [128, 8, 8])
        idxu8 = att.enter_context(nc.sbuf_tensor("idxu8", [128, 2, 8], mybir.dt.uint32))
        carry = asb("carry", [8, 1])
        ebias = asb("ebias", [128, 8])
        tkd = asb("tkd", [128, 32])
        ps = [att.enter_context(nc.psum_tensor(f"ps{i}", [128, 512], F32)) for i in range(8)]
        statm = reci[0:1, 0, 0:512]
        statm2 = reci[0:1, 0, 512:1024]
        statv = reci[0:1, 0, 512:1024]

        P.op("vector", lambda e, S: e.memset(sume[:], 1.0), inc="cIN")
        P.op("vector", lambda e, S: e.memset(carry[:], 0.0), inc="cIN")
        P.op("gpsimd", lambda e, S: e.iota(ebias[:], [[CAP, 8]], base=0,
             channel_multiplier=0, allow_small_or_imprecise_dtypes=True), inc="cWS")
        P.op("vector", lambda e, S: e.memset(xts[:], 0.0), inc="cIN")
        P.op("vector", lambda e, S: e.memset(rsg[:], 0), inc="cIN")
        dma("sync", tkd[:], tkdd, "sC")
        CW[1] = P.cur("cIN")
        ztot = E * CAP * RW
        zchunk = 128 * 4096
        xts_z16 = xts[:].bitcast(mybir.dt.uint16)
        zoff = 0
        while zoff < ztot:
            n_el = min(zchunk, ztot - zoff)
            npart = n_el // 4096
            P.op("scalar", lambda e, S, zoff=zoff, npart=npart: e.dma_start(
                out=xbuf_z[zoff:zoff + npart * 4096].rearrange("(p n) -> p n", n=4096),
                in_=xts_z16[0:npart, :, :]),
                waits=[CW[1]], inc="sZB", n=16)
            zoff += npart * 4096
        ZB = P.cur("sZB")

        slot_free = {}

        def LN_stats(xsrc_chunk, first_waits, snt):
            st_base = P.cnt["cST"]
            sq_base = P.cnt["cSQ"]
            for kc in range(8):
                sl = kc % 2
                P.op("scalar", lambda e, S, kc=kc, sl=sl, snt=snt: e.activation(
                    vsb[:, sl, snt, :], xsrc_chunk(kc), AF.Square),
                    waits=(first_waits if kc == 0 else []) +
                          [("cQE", None), ("cKE", None)] +
                          ([("cST", st_base)] if kc == 0 else
                           [("cST", st_base + kc - 1)] if kc >= 2 else []),
                    inc="cSQ")
                P.op("tensor", lambda e, S, kc=kc: e.matmul(
                    ps[6][0:1, :], onc[:], xsrc_chunk(kc),
                    start=(kc == 0), stop=(kc == 7)),
                    waits=(first_waits if kc == 0 else []) +
                          ([("cSQ", None), ("cSV", None), ("cNM", None), CW[1]]
                           if kc == 0 else []))
                P.op("tensor", lambda e, S, kc=kc, sl=sl, snt=snt: e.matmul(
                    ps[7][0:1, :], onc[:], vsb[:, sl, snt, :],
                    start=(kc == 0), stop=(kc == 7)),
                    waits=[("cSQ", sq_base + kc + 1)], inc="cST")
            smm = ("cST", st_base + 8)
            P.op("vector", lambda e, S: e.tensor_scalar(
                out=statm, in0=ps[6][0:1, :], scalar1=1.0 / D, scalar2=None,
                op0=AL.mult),
                waits=[smm, ("cQP", None), ("cKV", None), ("cBC", None),
                       ("cXN", None)], inc="cSV")
            P.op("scalar", lambda e, S: e.activation(statm2, statm, AF.Square),
                 waits=[("cSV", None)], inc="cSQ")
            P.op("vector", lambda e, S: e.scalar_tensor_tensor(
                out=statv, in0=ps[7][0:1, :], scalar=1.0 / D, in1=statm2,
                op0=AL.mult, op1=AL.subtract), waits=[("cSQ", None)], inc="cSV")
            P.op("tensor", lambda e, S: e.matmul(
                ps[6][:, :], onr[:], statm, start=True, stop=True),
                waits=[("cSV", None), ("cSQ", None)])
            P.op("tensor", lambda e, S: e.matmul(
                ps[7][:, :], onr[:], statv, start=True, stop=True), inc="cBC")
            bmm = P.cur("cBC")
            P.op("scalar", lambda e, S: e.copy(meanb[:], ps[6][:, :]),
                 waits=[bmm, ("cXN", None)], inc="cSQ")
            P.op("scalar", lambda e, S: e.activation(
                rstdb[:], ps[7][:, :], AF.Sqrt, bias=epsb[:, 0:1]),
                waits=[bmm, ("cXN", None)], inc="cSQ")
            P.op("vector", lambda e, S: e.reciprocal(rstdb[:], rstdb[:]),
                 waits=[("cSQ", None)], inc="cNM")
            return [P.cur("cSQ"), P.cur("cNM")]

        def xn_make(xsrc, dst, ln_rdy, extra, snt):
            P.op("vector", lambda e, S: e.tensor_tensor(
                tmp3[:, 2, :], xsrc, meanb[:], op=AL.subtract),
                waits=ln_rdy + extra, inc="cXN")
            P.op("vector", lambda e, S: e.tensor_tensor(
                dst, tmp3[:, 2, :], rstdb[:], op=AL.mult), inc="cXN")
            return P.cur("cXN")

        def rope(eS, oS, waits0):
            P.op("vector", lambda e, S: e.tensor_tensor(
                tmp3[:, 0, :], eS, csb[:, 1, :], op=AL.mult), waits=waits0)
            P.op("vector", lambda e, S: e.tensor_tensor(
                tmp3[:, 1, :], eS, csb[:, 0, :], op=AL.mult))
            P.op("vector", lambda e, S: e.tensor_tensor(
                tmp3[:, 2, :], oS, csb[:, 1, :], op=AL.mult))
            P.op("vector", lambda e, S: e.tensor_tensor(
                eS, tmp3[:, 1, :], tmp3[:, 2, :], op=AL.subtract))
            P.op("vector", lambda e, S: e.tensor_tensor(
                tmp3[:, 1, :], oS, csb[:, 0, :], op=AL.mult))

            def fin(e, S):
                return e.tensor_tensor(oS, tmp3[:, 0, :], tmp3[:, 1, :], op=AL.add)
            return fin

        for qb in range(NQB):
            seq, qbl = qb // 2, qb % 2
            qc0 = qb * QBT
            sc0 = qbl * QBT

            P.op("gpsimd", lambda e, S, c=qc0: e.dma_start(
                out=xq[:], in_=xin_r[:, :, c:c + QBT]),
                waits=[("cXT", None), ("cST", None), ("cXN", None)],
                inc="sX", n=16)
            xrdy = P.cur("sX")

            ln_rdy_nt = {}
            for nt in range(2):
                ntc = nt * 512
                dma("sync", csb[:, 0, :], cosd[:, sc0 + ntc:sc0 + ntc + 512], "sCS",
                    waits=[("cQE", None), ("cKE", None)])
                dma("sync", csb[:, 1, :], sind[:, sc0 + ntc:sc0 + ntc + 512], "sCS")
                cs_rdy = P.cur("sCS")

                if nt == 0:
                    ln_rdy_nt[0] = LN_stats(
                        lambda kc, ntc=ntc: xq[:, kc, ntc:ntc + 512], [xrdy], 0)
                ln_rdy = ln_rdy_nt[nt]

                for kc in range(8):
                    sl = kc % 2
                    wwl = [slot_free[("wq", sl)]] if ("wq", sl) in slot_free else []
                    dma("sync", wqs[:, sl, :, :], wq_r[:, kc, :], "sWQ", waits=wwl)
                    wrdy = P.cur("sWQ")
                    xn_rdy = xn_make(xq[:, kc, ntc:ntc + 512], xnt[:, sl, :], ln_rdy,
                                     [("cQP", P.cnt["cQP"] - 1 if kc >= 2 else 0)], nt)
                    for half in range(2):
                        for c in range(4):
                            last = (half == 1 and c == 3)
                            P.op("tensor", lambda e, S, kc=kc, sl=sl, half=half, c=c: e.matmul(
                                ps[2 * c + half][:, :], wqs[:, sl, 4 * half + c, :],
                                xnt[:, sl, :], start=(kc == 0), stop=(kc == 7)),
                                waits=[wrdy, xn_rdy] +
                                      ([("cQE", None), ("cKE", None), ("cVE", None),
                                        ("cVT", None), ("cVv", None)] if kc == 0 else []),
                                inc=("cQP" if last else None))
                    slot_free[("wq", sl)] = P.cur("cQP")
                qp_mm = P.cur("cQP")
                for c in range(4):
                    P.op("vector", lambda e, S, c=c, ntc=ntc: e.tensor_scalar(
                        out=qa[:, c, ntc:ntc + 512], in0=ps[2 * c][:, :],
                        scalar1=bq_sb[:, c:c + 1], scalar2=None, op0=AL.add),
                        waits=[qp_mm, cs_rdy, ("sSP", None)], inc="cQE")
                    P.op("vector", lambda e, S, c=c, ntc=ntc: e.tensor_scalar(
                        out=qa[:, 4 + c, ntc:ntc + 512], in0=ps[2 * c + 1][:, :],
                        scalar1=bq_sb[:, 4 + c:5 + c], scalar2=None, op0=AL.add),
                        inc="cQE")
                    fin = rope(qa[:, c, ntc:ntc + 512], qa[:, 4 + c, ntc:ntc + 512], [])
                    P.op("vector", fin, inc="cQE")
                q_rope = P.cur("cQE")

                for kc in range(8):
                    sl = kc % 2
                    wwl = [slot_free[("wkv", sl)]] if ("wkv", sl) in slot_free else []
                    dma("sync", wkvs[:, sl, 0:2, :], wk_r[:, kc, :], "sWKV", waits=wwl)
                    dma("sync", wkvs[:, sl, 2:4, :], wv_r[:, kc, :], "sWKV")
                    wrdy = P.cur("sWKV")
                    xn_rdy = xn_make(xq[:, kc, ntc:ntc + 512], xnt[:, sl, :], ln_rdy,
                                     [q_rope, ("cKV", P.cnt["cKV"] - 1 if kc >= 2 else 0)], nt)
                    for m in range(4):
                        P.op("tensor", lambda e, S, kc=kc, sl=sl, m=m: e.matmul(
                            ps[m][:, :], wkvs[:, sl, m, :], xnt[:, sl, :],
                            start=(kc == 0), stop=(kc == 7)),
                            waits=[wrdy, xn_rdy, qp_mm] +
                                  ([("cKE", None), ("cVE", None)] if kc == 0 else []),
                            inc=("cKV" if m == 3 else None))
                    slot_free[("wkv", sl)] = P.cur("cKV")
                kv_mm = P.cur("cKV")
                kcol = sc0 + ntc
                for half in range(2):
                    P.op("vector", lambda e, S, half=half, kcol=kcol: e.tensor_scalar(
                        out=ksb[:, half, kcol:kcol + 512], in0=ps[half][:, :],
                        scalar1=bkv_sb[:, half:half + 1], scalar2=None, op0=AL.add),
                        waits=[kv_mm, cs_rdy, ("cKR", None)], inc="cKE")
                fin = rope(ksb[:, 0, kcol:kcol + 512], ksb[:, 1, kcol:kcol + 512],
                           [("cKE", None)])
                P.op("vector", fin, inc="cKE")
                b = nt % 2
                for ch in range(2):
                    P.op("vector", lambda e, S, ch=ch, b=b: e.tensor_scalar(
                        out=vsb[:, ch, b, :], in0=ps[2 + ch][:, :],
                        scalar1=bkv_sb[:, 2 + ch:3 + ch], scalar2=None, op0=AL.add),
                        waits=[kv_mm, ("cVT", None)], inc="cVE")
                v_ev = P.cur("cVE")
                if nt == 0:
                    ln_rdy_nt[1] = LN_stats(
                        lambda kc: xq[:, kc, 512:1024], [xrdy], 1)
                for ch in range(2):
                    for sub in range(4):
                        pb = 4 + ((ch * 4 + sub) % 2)
                        kt = (sc0 + ntc + sub * 128) // 128
                        P.op("tensor", lambda e, S, ch=ch, sub=sub, pb=pb, b=b: e.matmul(
                            ps[pb][:, 0:128], vsb[:, ch, b, sub * 128:sub * 128 + 128],
                            idn[:], is_transpose=True),
                            waits=[v_ev, CW[0], ("cVv", max(0, P.cnt["cVv"] - 1))],
                            inc="cVT")
                        vmm = P.cur("cVT")
                        P.op("scalar", lambda e, S, ch=ch, kt=kt, pb=pb: e.copy(
                            vtm[:, 2 * ch, kt, 0:64], ps[pb][:, 0:64]), waits=[vmm])
                        P.op("scalar", lambda e, S, ch=ch, kt=kt, pb=pb: e.copy(
                            vtm[:, 2 * ch + 1, kt, 0:64], ps[pb][:, 64:128]),
                            inc="cVv")
            kt0 = sc0 // 128
            P.op("vector", lambda e, S, kt0=kt0: e.memset(
                vtm[:, :, kt0:kt0 + 8, 64:65], 1.0), waits=[("cVT", None)], inc="cVE")
            vtm_rdy = [P.cur("cVT"), P.cur("cVv"), P.cur("cVE")]

            n_keys = (qbl + 1) * QBT
            for c in range(4):
                for j in range(4):
                    for half in range(2):
                        P.op("scalar", lambda e, S, j=j, half=half, c=c, n_keys=n_keys: e.copy(
                            krep[32 * j:32 * j + 32, half, 0:n_keys],
                            ksb[32 * c:32 * c + 32, half, 0:n_keys]),
                            waits=[("cKE", None), ("cSC", None)],
                            inc=("cKR" if (j == 3 and half == 1) else None))
                krep_rdy = P.cur("cKR")
                for qt in range(2):
                    qtc = qt * 512
                    n_kt = (sc0 + qtc + 512) // 128
                    for kt in range(n_kt):
                        for j in range(4):
                            for half in range(2):
                                last = (j == 3 and half == 1)
                                P.op("tensor", lambda e, S, j=j, half=half, kt=kt, c=c, qtc=qtc: e.matmul(
                                    ps[j][:, :],
                                    krep[32 * j:32 * j + 32, half, kt * 128:kt * 128 + 128],
                                    qa[32 * j:32 * j + 32, (c if half == 0 else 4 + c),
                                       qtc:qtc + 512],
                                    start=(half == 0), stop=(half == 1),
                                    tile_position=(32 * j, 0)),
                                    waits=([krep_rdy, q_rope, ("cEX", None),
                                            ("cNX", None)] if (j == 0 and half == 0) else []),
                                    inc=("cSC" if last else None))
                        sc_mm = P.cur("cSC")
                        joff = kt - (n_kt - 4)
                        if joff >= 0:
                            for j in range(4):
                                P.op("vector", lambda e, S, j=j, joff=joff: e.tensor_tensor(
                                    ps[j][:, :], ps[j][:, :], mskb[:, joff, :], op=AL.add),
                                    waits=([sc_mm, CW[0]] if j == 0 else []),
                                    inc=("cMK" if j == 3 else None))
                            exw = [P.cur("cMK")]
                        else:
                            exw = [sc_mm]
                        for j in range(4):
                            P.op("scalar", lambda e, S, j=j, kt=kt: e.activation(
                                expb[:, kt % 2, j, :], ps[j][:, :], AF.Exp, scale=0.125),
                                waits=(exw + [("cAV", max(0, P.cnt["cAV"] - 1))]) if j == 0 else [],
                                inc=("cEX" if j == 3 else None))
                        ex_rdy = P.cur("cEX")
                        for j in range(4):
                            P.op("tensor", lambda e, S, j=j, kt=kt, c=c, n_kt=n_kt: e.matmul(
                                ps[4 + j][0:65, :], vtm[:, c, kt, 0:65],
                                expb[:, kt % 2, j, :],
                                start=(kt == 0), stop=(kt == n_kt - 1)),
                                waits=([ex_rdy] + (vtm_rdy + [("cAE", None)] if kt == 0 else [])) if j == 0 else [],
                                inc=("cAV" if j == 3 else None))
                    av_done = P.cur("cAV")
                    for j in range(4):
                        P.op("scalar", lambda e, S, j=j, qtc=qtc: e.copy(
                            sume[32 * j:32 * j + 1, qtc:qtc + 512],
                            ps[4 + j][64:65, :]),
                            waits=([av_done, ("cRV", None), ("cNX", None)] if j == 0 else []),
                            inc=("cSE" if j == 3 else None))
                        P.op("scalar", lambda e, S, j=j, c=c, qtc=qtc: e.copy(
                            qa[32 * j:32 * j + 32, c, qtc:qtc + 512], ps[4 + j][0:32, :]))
                        P.op("scalar", lambda e, S, j=j, c=c, qtc=qtc: e.copy(
                            qa[32 * j:32 * j + 32, 4 + c, qtc:qtc + 512],
                            ps[4 + j][32:64, :]),
                            inc=("cAE" if j == 3 else None))
                se_rdy = P.cur("cSE")
                ae_rdy = P.cur("cAE")
                P.op("scalar", lambda e, S: e.activation(
                    reci[:, 0, :], sume[:, :], AF.Ln),
                    waits=[se_rdy, ("cNM", None)], inc="cRC")
                P.op("scalar", lambda e, S: e.activation(
                    reci[:, 0, :], reci[:, 0, :], AF.Exp, scale=-1.0), inc="cRC")
                nsc = xts[:, 0, 0:QBT]
                P.op("vector", lambda e, S, nsc=nsc: e.tensor_tensor(
                    nsc, sume[:, :], reci[:, 0, :], op=AL.mult),
                    waits=[("cRC", None), ("sXO", None), ("sZB", None)], inc="cRV")
                P.op("vector", lambda e, S, nsc=nsc: e.tensor_scalar(
                    out=nsc, in0=nsc, scalar1=-1.0, scalar2=2.0,
                    op0=AL.mult, op1=AL.add), inc="cRV")
                P.op("vector", lambda e, S, nsc=nsc: e.tensor_tensor(
                    sume[:, :], reci[:, 0, :], nsc, op=AL.mult), inc="cRV")
                rc_rdy = P.cur("cRV")
                for qt in range(2):
                    qtc = qt * 512
                    P.op("tensor", lambda e, S, qtc=qtc: e.matmul(
                        ps[0][:, :], selm[:], sume[:, qtc:qtc + 512],
                        start=True, stop=True),
                        waits=[rc_rdy, CW[0], ("cEX", None), ("cNX", None)], inc="cSC")
                    P.op("vector", lambda e, S: e.tensor_copy(recb[:], ps[0][:, :]),
                         waits=[("cSC", None)], inc="cNX")
                    for u in range(2):
                        P.op("vector", lambda e, S, c=c, u=u, qtc=qtc: e.tensor_tensor(
                            qa[:, c + 4 * u, qtc:qtc + 512],
                            qa[:, c + 4 * u, qtc:qtc + 512], recb[:], op=AL.mult),
                            waits=([ae_rdy] if u == 0 else []),
                            inc=("cNM" if u == 1 else None))
            nm_rdy = P.cur("cNM")

            for m in range(8):
                sl = m % 2
                wwl = [slot_free[("wo", sl)]] if ("wo", sl) in slot_free else []
                if ("wq", sl) in slot_free:
                    wwl.append(slot_free[("wq", sl)])
                dma("sync", wqs[:, sl, :, :], wo_r[:, :, m * 128:m * 128 + 128],
                    "sWO", waits=wwl)
                wrdy = P.cur("sWO")
                for nt in range(2):
                    pb = 4 + nt
                    for kc in range(8):
                        P.op("tensor", lambda e, S, m=m, sl=sl, nt=nt, kc=kc, pb=pb: e.matmul(
                            ps[pb][:, :], wqs[:, sl, kc, :],
                            qa[:, kc, nt * 512:nt * 512 + 512],
                            start=(kc == 0), stop=(kc == 7)),
                            waits=([wrdy, nm_rdy, ("cWO", max(0, P.cnt["cWO"] - 1))]
                                   if kc == 0 else []),
                            inc=("cG" if kc == 7 else None))
                    womm = P.cur("cG")
                    P.op("vector", lambda e, S, m=m, nt=nt, pb=pb: e.scalar_tensor_tensor(
                        out=xq[:, m, nt * 512:nt * 512 + 512], in0=ps[pb][:, :],
                        scalar=bo_sb[:, m:m + 1],
                        in1=xq[:, m, nt * 512:nt * 512 + 512],
                        op0=AL.add, op1=AL.add), waits=[womm], inc="cWO")
                slot_free[("wo", sl)] = P.cur("cWO")
                slot_free[("wq", sl)] = P.cur("cG")
            xp_rdy = P.cur("cWO")

            ln2_rdy_nt = {0: LN_stats(lambda kc: xq[:, kc, 0:512], [xp_rdy], 0)}
            for nt in range(2):
                ntc = nt * 512
                ln_rdy = ln2_rdy_nt[nt]
                for kc in range(8):
                    xn_rdy = xn_make(xq[:, kc, ntc:ntc + 512], qa[:, kc, ntc:ntc + 512],
                                     ln_rdy, [("cG", None), ("sSP", None)], nt)
                    P.op("tensor", lambda e, S, kc=kc, ntc=ntc: e.matmul(
                        ps[5][0:8, :], gw_sb[:, kc, :], qa[:, kc, ntc:ntc + 512],
                        start=(kc == 0), stop=(kc == 7)),
                        waits=[xn_rdy] + ([CW[0], xp_rdy, ("cGT", None)] if kc == 0 else []),
                        inc=("cG" if kc == 7 else None))
                if nt == 0:
                    ln2_rdy_nt[1] = LN_stats(
                        lambda kc: xq[:, kc, 512:1024], [xp_rdy], 1)
                P.op("vector", lambda e, S, ntc=ntc: e.tensor_scalar(
                    out=lgq[0:8, ntc:ntc + 512], in0=ps[5][0:8, :],
                    scalar1=gb_sb[0:8, 0:1], scalar2=None, op0=AL.add),
                    waits=[("cG", None), ("cGT", None)], inc="cGT")
            lg_rdy = P.cur("cGT")

            def gop(eng, fn, w=()):
                P.op(eng, fn, waits=list(w) + [("cGT", None)], inc="cGT")
            for st_ in range(8):
                gop("tensor", lambda e, S, st_=st_: e.matmul(
                    ps[4][:, 8 * st_:8 * st_ + 8], lgq[0:8, 128 * st_:128 * st_ + 128],
                    idn[0:8, 0:8], is_transpose=True), w=[lg_rdy, ("cAE", None)])
            gop("vector", lambda e, S: e.tensor_copy(gsct[:, :], ps[4][:, 0:64]))
            lgT = gsct.ap() if hasattr(gsct, "ap") else gsct
            gop("vector", lambda e, S: e.tensor_reduce(
                gsm[:, 0, :], gsct[:].rearrange("p (s e) -> p s e", e=8),
                mybir.AxisListType.X, AL.max))
            mxb = gsm[:, 0:1, :].rearrange("p a e -> p e a").broadcast_to((128, 8, 8))
            m2b = gsm[:, 1:2, :].rearrange("p a e -> p e a").broadcast_to((128, 8, 8))
            gop("vector", lambda e, S: e.tensor_tensor(
                gsc2[:, 1, :].rearrange("p (s e) -> p s e", e=8),
                gsct[:].rearrange("p (s e) -> p s e", e=8), mxb, op=AL.is_equal),
                w=[("cRH", None)])
            gop("vector", lambda e, S: e.scalar_tensor_tensor(
                out=gsc2[:, 1, :], in0=gsc2[:, 1, :], scalar=-1e30,
                in1=gsct[:], op0=AL.mult, op1=AL.add))
            gop("vector", lambda e, S: e.tensor_reduce(
                gsm[:, 1, :], gsc2[:, 1, :].rearrange("p (s e) -> p s e", e=8),
                mybir.AxisListType.X, AL.max))
            gop("vector", lambda e, S: e.tensor_tensor(
                gsc2[:, 0, :].rearrange("p (s e) -> p s e", e=8),
                gsct[:].rearrange("p (s e) -> p s e", e=8), mxb, op=AL.subtract))
            gop("scalar", lambda e, S: e.activation(
                gsc2[:, 0, :], gsc2[:, 0, :], AF.Exp))
            gop("vector", lambda e, S: e.tensor_reduce(
                gsm[:, 2, :], gsc2[:, 0, :].rearrange("p (s e) -> p s e", e=8),
                mybir.AxisListType.X, AL.add))
            gop("vector", lambda e, S: e.reciprocal(gsm[:, 3, :], gsm[:, 2, :]))
            rcb = gsm[:, 3:4, :].rearrange("p a e -> p e a").broadcast_to((128, 8, 8))
            gop("vector", lambda e, S: e.tensor_tensor(
                gsc2[:, 0, :].rearrange("p (s e) -> p s e", e=8),
                gsc2[:, 0, :].rearrange("p (s e) -> p s e", e=8), rcb, op=AL.mult))
            gop("vector", lambda e, S: e.tensor_tensor(
                gsc2[:, 1, :].rearrange("p (s e) -> p s e", e=8),
                gsct[:].rearrange("p (s e) -> p s e", e=8), m2b, op=AL.is_ge))
            gop("vector", lambda e, S: e.tensor_tensor(
                gsc2[:, 1, :], gsc2[:, 0, :], gsc2[:, 1, :], op=AL.mult))
            for st_ in range(8):
                pb = 4 + st_ // 4
                gop("tensor", lambda e, S, st_=st_, pb=pb: e.matmul(
                    ps[pb][0:8, 128 * (st_ % 4):128 * (st_ % 4) + 128],
                    gsc2[:, 1, 8 * st_:8 * st_ + 8], idn[:], is_transpose=True))
            for st_ in range(8):
                pb = 4 + st_ // 4
                gop("vector", lambda e, S, st_=st_, pb=pb: e.tensor_copy(
                    lgq[0:8, 128 * st_:128 * st_ + 128],
                    ps[pb][0:8, 128 * (st_ % 4):128 * (st_ % 4) + 128]))
            rnk = reci[0:8, 0, 0:QBT]
            rsel = xts[0:8, 1, 0:QBT]
            ones8 = xnt[0:8, :, :].rearrange("p a b -> p (a b)")
            gop("vector", lambda e, S: e.memset(ones8, 1.0), w=[("cKV", None)])
            gop("vector", lambda e, S: e.tensor_scalar(
                out=rsel, in0=lgq[0:8, :], scalar1=0.0, scalar2=None, op0=AL.is_gt),
                w=[("cRC", None), ("sXO", None)])
            gop("vector", lambda e, S: e.tensor_tensor_scan(
                rnk, ones8, rsel, carry[0:8, 0:1], op0=AL.mult, op1=AL.add))
            gop("vector", lambda e, S: e.tensor_copy(carry[0:8, 0:1], rnk[:, QBT - 1:QBT]))
            gop("vector", lambda e, S: e.scalar_tensor_tensor(
                out=rsel, in0=rsel, scalar=-4194304.0, in1=rnk, op0=AL.mult, op1=AL.add))
            gop("vector", lambda e, S: e.tensor_scalar(
                out=rsel, in0=rsel, scalar1=4194303.0, scalar2=None, op0=AL.add))
            for st_ in range(8):
                gop("tensor", lambda e, S, st_=st_: e.matmul(
                    ps[6][:, 8 * st_:8 * st_ + 8], rsel[:, 128 * st_:128 * st_ + 128],
                    idn[0:8, 0:8], is_transpose=True))
            gop("vector", lambda e, S: e.tensor_copy(sidxT[:].rearrange("p a b -> p (a b)"),
                                                     ps[6][:, 0:64]))

            sxo_tt = {}
            for tt in range(8):
                sl = tt % 2
                for kc in range(8):
                    pb = 2 + ((tt * 8 + kc) % 2)
                    P.op("tensor", lambda e, S, tt=tt, kc=kc, pb=pb: e.matmul(
                        ps[pb][:, 0:128], xq[:, kc, tt * 128:tt * 128 + 128], idn[:],
                        is_transpose=True),
                        waits=([xp_rdy, ("cEX", None)] if (tt == 0 and kc == 0) else []) +
                              [("cXE", max(0, P.cnt["cXE"] - 1))],
                        inc="cXT")
                    P.op("scalar", lambda e, S, tt=tt, kc=kc, pb=pb, sl=sl: e.copy(
                        xts[:, sl, kc * 128:kc * 128 + 128], ps[pb][:, 0:128]),
                        waits=[P.cur("cXT")] +
                              ([("cGT", None), ("cRV", None)] if (kc == 0 and tt == 0) else []) +
                              ([sxo_tt[tt - 2]] if (kc == 0 and tt - 2 in sxo_tt) else []),
                        inc="cXE")
                P.op("gpsimd", lambda e, S, tt=tt, sl=sl, c=qc0: e.dma_start(
                    out=out[c + tt * 128:c + tt * 128 + 128, :], in_=xts[:, sl, :]),
                    waits=[("cXE", None)], inc="sXO", n=16)
                sxo_tt[tt] = P.cur("sXO")

            rsg_b = rsg[:].bitcast(mybir.dt.bfloat16)
            rsg_f = rsg[:].bitcast(F32)
            rsg_free = {}
            for tt in range(8):
                sl = tt % 2
                for kc in range(8):
                    pb = (tt * 8 + kc) % 2
                    P.op("tensor", lambda e, S, tt=tt, kc=kc, pb=pb: e.matmul(
                        ps[pb][:, 0:128], qa[:, kc, tt * 128:tt * 128 + 128], idn[:],
                        is_transpose=True),
                        waits=([("cG", None), ("cNX", None), ("cXN", None),
                                ("cGT", None)] if (tt == 0 and kc == 0) else []) +
                              [("cRL", max(0, P.cnt["cRL"] - 1))],
                        inc="cRT")
                    P.op("scalar", lambda e, S, kc=kc, pb=pb, sl=sl: e.activation(
                        rsg_b[:, sl, kc * 128:kc * 128 + 128], ps[pb][:, 0:128], AF.Copy),
                        waits=[P.cur("cRT")] +
                              ([rsg_free[tt - 2]] if (kc == 0 and tt - 2 in rsg_free) else []),
                        inc="cRH")
                    P.op("vector", lambda e, S, kc=kc, pb=pb, sl=sl: e.tensor_tensor(
                        rsg_b[:, sl, 1024 + kc * 128:1024 + kc * 128 + 128], ps[pb][:, 0:128],
                        rsg_b[:, sl, kc * 128:kc * 128 + 128], op=AL.subtract),
                        waits=[P.cur("cRH")], inc="cRL")
                P.op("scalar", lambda e, S, tt=tt, sl=sl: e.copy(
                    rsg_f[:, sl, 1024:1032], gsc2[:, 1, 8 * tt:8 * tt + 8]),
                    waits=[("cGT", None)], inc="cRH")
                P.op("scalar", lambda e, S, tt=tt, sl=sl, qb=qb: e.copy(
                    rsg_f[:, sl, 1032:1033], tkd[:, qb * 8 + tt:qb * 8 + tt + 1]),
                    waits=[("sC", None)], inc="cRH")
                P.op("vector", lambda e, S, tt=tt, sl=sl: e.tensor_tensor(
                    idxu8[:, sl, :], sidxT[:, tt, :], ebias[:], op=AL.add),
                    waits=[("cGT", None), ("cWS", 1)] +
                          ([rsg_free[tt - 2]] if tt - 2 in rsg_free else []),
                    inc="cRL")
                xbuf_rows = xbuf.rearrange("e c w -> (e c) w")
                for e_x in range(E):
                    def _scat(e, S, e_x=e_x, sl=sl, xr=xbuf_rows):
                        return e.indirect_dma_start(
                            out=xr,
                            out_offset=bass.IndirectOffsetOnAxis(ap=idxu8[:, sl, e_x:e_x + 1], axis=0),
                            in_=rsg[:, sl, :], in_offset=None,
                            bounds_check=S["bndx"][e_x], oob_is_err=False)
                    P.op("gpsimd", _scat,
                        waits=[("cRH", None), ("cRL", None)] +
                              ([("sZB", ZB[1])] if (qb == 0 and tt == 0 and e_x == 0) else []),
                        inc="sSP", n=16)
                rsg_free[tt] = P.cur("sSP")

        P.op("tensor", lambda e, S: e.matmul(
            ps[0][0:1, 0:1], onc[0:1, :], onc[0:1, :], start=True, stop=True),
            waits=[("cSC", None), ("cG", None), ("cNX", None), ("cRH", None),
                   ("cRL", None), ("cXE", None)], inc="phA")
        P.op("vector", lambda e, S: e.memset(tmp3[0:1, 0, 0:1], 0.0),
             waits=[("cGT", None), ("cNM", None), ("cWO", None), ("cXN", None),
                    ("cQE", None), ("cKE", None), ("cVE", None)], inc="phA")
        P.op("scalar", lambda e, S: e.copy(tmp3[0:1, 1, 0:1], tmp3[0:1, 0, 0:1]),
             waits=[("cGT", None), ("cSE", None), ("cAE", None)], inc="phA")
        att.close()
        PH = [("phA", P.cnt["phA"]), ("sSP", P.cnt["sSP"]), ("sXO", P.cnt["sXO"]),
              ("sGB", P.cnt["sGB"]), ("sGR", P.cnt["sGR"])]

        BF = mybir.dt.bfloat16
        moe = ExitStack()
        msb = lambda name, shape, dt=F32: moe.enter_context(nc.sbuf_tensor(name, shape, dt))
        y_acc = msb("y_acc", [128, 8, THT])
        w1b = msb("w1b", [128, 2, 2, 8, 512], BF)
        w2b = msb("w2b", [128, 2, 2, 4, 1024], BF)
        wf = msb("wf", [128, 4096])
        xnb = msb("xnb", [128, 2, 8, THT], BF)
        h32 = msb("h32", [128, 2, 256])
        hb = msb("hb", [128, 2, 2, 256], BF)
        xrows = moe.enter_context(nc.sbuf_tensor("xrows", [128, 6, RW], mybir.dt.uint16))
        ystage = msb("ystage", [128, 2, 1024])
        gcol = msb("gcol", [128, 6])
        idtu = moe.enter_context(nc.sbuf_tensor("idtu", [128, 6], mybir.dt.uint32))
        idnb = moe.enter_context(nc.sbuf_tensor("idnb", [128, 128], BF))
        psy = moe.enter_context(nc.psum_tensor("psy", [128, 2048], F32))
        psh = moe.enter_context(nc.psum_tensor("psh", [128, 512], F32))
        pstA = moe.enter_context(nc.psum_tensor("pstA", [128, 512], F32))
        pstB = moe.enter_context(nc.psum_tensor("pstB", [128, 512], F32))
        pst = [pstA, pstB]
        xrows_b = xrows[:].bitcast(BF)
        xrows_f = xrows[:].bitcast(F32)
        wf1v = wf[:].rearrange("p (kc m) -> p kc m", kc=8)
        wf2v = wf[:].rearrange("p (mc m) -> p mc m", mc=4)

        P.op("vector", lambda e, S: e.tensor_copy(idnb[:], idn[:]),
             waits=[CW[0]] + list(PH), inc="cYS")

        SCT = ("sSP", P.cnt["sSP"])
        ch_free = {}
        sad_col = {}
        for e_i in range(E):
            for half in range(2):
                base = half * 6
                xn_rdy_nt = []
                for rc in range(6):
                    gw_ = [SCT] + list(PH) + ([ch_free[rc]] if rc in ch_free else [])
                    P.op("sync", lambda e, S, e_i=e_i, rc=rc, base=base: e.dma_start(
                        out=xrows[:, rc, :], in_=xbuf_g[e_i, :, base + rc, :]),
                        waits=gw_, inc="sGA", n=16)
                    ga_rdy = P.cur("sGA")
                    for kc in range(8):
                        for hl in range(2):
                            pb = (rc * 16 + kc * 2 + hl) % 2
                            P.op("tensor", lambda e, S, rc=rc, kc=kc, hl=hl, pb=pb: e.matmul(
                                pst[pb][:, 0:64].bitcast(BF),
                                xrows_b[:, rc, hl * 1024 + kc * 128:hl * 1024 + kc * 128 + 128],
                                idnb[:], is_transpose=True),
                                waits=[ga_rdy, ("cYS", None)] +
                                      [("cXG", max(0, P.cnt["cXG"] - 1))],
                                inc="cXT")
                            P.op("vector", lambda e, S, rc=rc, kc=kc, hl=hl, pb=pb: e.tensor_copy(
                                xnb[:, hl, kc, rc * 128:rc * 128 + 128],
                                pst[pb][:, 0:64].bitcast(BF)),
                                waits=[P.cur("cXT")], inc="cXG")
                    P.op("scalar", lambda e, S, e_i=e_i, rc=rc: e.copy(
                        gcol[:, rc:rc + 1], xrows_f[:, rc, 1024 + e_i:1025 + e_i]),
                        waits=[ga_rdy, ("cYS", None), ("cXF", max(0, P.cnt["cXF"] - 15))], inc="cXF")
                    P.op("vector", lambda e, S, rc=rc: e.tensor_copy(
                        idtu[:, rc:rc + 1], xrows_f[:, rc, 1032:1033]),
                        waits=[ga_rdy] + ([sad_col[rc]] if rc in sad_col else []),
                        inc="cXG")
                    xn_rdy_nt.append([P.cur("cXG"), P.cur("cXF")])
                xn_rdy_nt = [xn_rdy_nt[2 * i + 1][0] for i in range(3)]
                ye_base_box = [P.cnt["cYE"] - 8]
                for mg in range(8):
                    sl = mg % 2
                    sfw = slot_free.get(("w1", sl), [])
                    if half == 0:
                        P.op("sync", lambda e, S, e_i=e_i, mg=mg: e.dma_start(
                            out=wf[:], in_=w1_r[e_i, :, :, mg * 512:mg * 512 + 512]),
                            waits=list(PH) + [("cWS", None)], inc="sW1", n=16)
                        w1f_rdy = P.cur("sW1")
                        P.op("gpsimd", lambda e, S, sl=sl: e.tensor_copy(
                            w1b[:, sl, 0, :, :], wf1v),
                            waits=[w1f_rdy] + sfw, inc="cWS")
                        P.op("gpsimd", lambda e, S, sl=sl: e.tensor_tensor(
                            w1b[:, sl, 1, :, :], wf1v, w1b[:, sl, 0, :, :],
                            op=AL.subtract), waits=[P.cur("cWS")], inc="cWS")
                        w1_rdy = ("cWS", P.cnt["cWS"])
                        P.op("sync", lambda e, S, e_i=e_i, mg=mg, sl=sl: e.dma_start(
                            out=w1hd[e_i, mg], in_=w1b[:, sl, 0, :, :]),
                            waits=[w1_rdy], inc="sWB", n=16)
                        P.op("sync", lambda e, S, e_i=e_i, mg=mg, sl=sl: e.dma_start(
                            out=w1ld[e_i, mg], in_=w1b[:, sl, 1, :, :]),
                            inc="sWB", n=16)
                        P.op("sync", lambda e, S, e_i=e_i, mg=mg: e.dma_start(
                            out=wf[:], in_=w2_r[e_i, :, 4 * mg:4 * mg + 4, :]),
                            waits=[("cWS", None)], inc="sW2", n=16)
                        w2f_rdy = P.cur("sW2")
                        P.op("gpsimd", lambda e, S, sl=sl: e.tensor_copy(
                            w2b[:, sl, 0, :, :], wf2v),
                            waits=[w2f_rdy] + slot_free.get(("w2", sl), []), inc="cWS")
                        P.op("gpsimd", lambda e, S, sl=sl: e.tensor_tensor(
                            w2b[:, sl, 1, :, :], wf2v, w2b[:, sl, 0, :, :],
                            op=AL.subtract), waits=[P.cur("cWS")], inc="cWS")
                        w2_rdy = ("cWS", P.cnt["cWS"])
                        P.op("sync", lambda e, S, e_i=e_i, mg=mg, sl=sl: e.dma_start(
                            out=w2hd[e_i, mg], in_=w2b[:, sl, 0, :, :]),
                            waits=[w2_rdy], inc="sWB", n=16)
                        P.op("sync", lambda e, S, e_i=e_i, mg=mg, sl=sl: e.dma_start(
                            out=w2ld[e_i, mg], in_=w2b[:, sl, 1, :, :]),
                            inc="sWB", n=16)
                        slot_free[("w1", sl)] = [("sWB", P.cnt["sWB"] - 32)]
                        slot_free[("w2", sl)] = [("sWB", P.cnt["sWB"])]
                    else:
                        P.op("sync", lambda e, S, e_i=e_i, mg=mg, sl=sl: e.dma_start(
                            out=w1b[:, sl, 0, :, :], in_=w1hd[e_i, mg]),
                            waits=list(PH) + sfw, inc="sW1", n=16)
                        P.op("sync", lambda e, S, e_i=e_i, mg=mg, sl=sl: e.dma_start(
                            out=w1b[:, sl, 1, :, :], in_=w1ld[e_i, mg]),
                            inc="sW1", n=16)
                        w1_rdy = P.cur("sW1")
                        P.op("sync", lambda e, S, e_i=e_i, mg=mg, sl=sl: e.dma_start(
                            out=w2b[:, sl, 0, :, :], in_=w2hd[e_i, mg]),
                            waits=slot_free.get(("w2", sl), []), inc="sW2", n=16)
                        P.op("sync", lambda e, S, e_i=e_i, mg=mg, sl=sl: e.dma_start(
                            out=w2b[:, sl, 1, :, :], in_=w2ld[e_i, mg]),
                            inc="sW2", n=16)
                        w2_rdy = P.cur("sW2")
                    for nt in range(3):
                        ntc = nt * 256

                        def emit_h(m, nt=nt, ntc=ntc, sl=sl, w1_rdy=w1_rdy):
                            hs = (nt * 4 + m) % 2
                            for kc in range(8):
                                for hp in range(3):
                                    wi, xi = ((0, 0), (1, 0), (0, 1))[hp]
                                    P.op("tensor", lambda e, S, sl=sl, m=m, kc=kc, ntc=ntc, hs=hs, wi=wi, xi=xi, hp=hp: e.matmul(
                                        psh[:, hs * 256:hs * 256 + 256],
                                        w1b[:, sl, wi, kc, m * 128:m * 128 + 128],
                                        xnb[:, xi, kc, ntc:ntc + 256],
                                        start=(kc == 0 and hp == 0),
                                        stop=(kc == 7 and hp == 2),
                                        skip_group_check=True),
                                        waits=([w1_rdy, xn_rdy_nt[nt],
                                                ("cHE", None)]
                                               if (kc == 0 and hp == 0) else []),
                                        inc=("cH" if (kc == 7 and hp == 2) else None))
                            return P.cur("cH")

                        def emit_hpost(m, h_mm, nt=nt, e_i=e_i, mg=mg):
                            hs = (nt * 4 + m) % 2
                            P.op("scalar", lambda e, S, e_i=e_i, mg=mg, m=m, hs=hs: e.activation(
                                h32[:, hs, :], psh[:, hs * 256:hs * 256 + 256],
                                AF.Relu, bias=b1_sb[:, e_i, mg * 4 + m:mg * 4 + m + 1]),
                                waits=[h_mm, ("cHL", max(0, P.cnt["cHL"] - 1))], inc="cSQ")
                            P.op("scalar", lambda e, S, hs=hs: e.activation(
                                hb[:, hs, 0, :], h32[:, hs, :], AF.Copy),
                                waits=[P.cur("cSQ"), ("cY", max(0, P.cnt["cY"] - 1))], inc="cHE")
                            he_rdy = P.cur("cHE")
                            P.op("vector", lambda e, S, hs=hs: e.tensor_tensor(
                                hb[:, hs, 1, :], h32[:, hs, :], hb[:, hs, 0, :],
                                op=AL.subtract),
                                waits=[he_rdy], inc="cHL")
                            return he_rdy, P.cur("cHL")

                        def emit_y(m, he_rdy, hl_rdy, nt=nt, sl=sl, w2_rdy=w2_rdy):
                            hs = (nt * 4 + m) % 2
                            for j in range(8):
                                for yp in range(3):
                                    wi, hi_ = ((0, 0), (1, 0), (0, 1))[yp]
                                    P.op("tensor", lambda e, S, sl=sl, m=m, j=j, hs=hs, wi=wi, hi_=hi_, yp=yp: e.matmul(
                                        psy[:, j * 256:j * 256 + 256],
                                        w2b[:, sl, wi, m, j * 128:j * 128 + 128],
                                        hb[:, hs, hi_, :],
                                        start=(m == 0 and j % 2 == 0 and yp == 0),
                                        stop=(m == 3 and yp == 2),
                                        skip_group_check=True),
                                        waits=([w2_rdy, he_rdy, hl_rdy] if (j == 0 and yp == 0) else []) +
                                              ([("cYE", ye_base_box[0] + (j | 1) + 1)]
                                               if (m == 0 and yp == 0) else []),
                                        inc=("cY" if (j == 7 and yp == 2) else None))

                        h_mm = emit_h(0)
                        hpost = emit_hpost(0, h_mm)
                        for m in range(4):
                            if m < 3:
                                nxt_mm = emit_h(m + 1)
                            emit_y(m, hpost[0], hpost[1])
                            if m < 3:
                                hpost = emit_hpost(m + 1, nxt_mm)
                        y_pair = P.cur("cY")
                        ye_base_box[0] = P.cnt["cYE"]
                        for j in range(8):
                            if mg == 0:
                                P.op("vector", lambda e, S, j=j, ntc=ntc, e_i=e_i: e.tensor_scalar(
                                    out=y_acc[:, j, ntc:ntc + 256],
                                    in0=psy[:, j * 256:j * 256 + 256],
                                    scalar1=b2_sb[:, e_i, j:j + 1], scalar2=None,
                                    op0=AL.add),
                                    waits=[y_pair] + ([("cYS", None)] if j == 0 else []),
                                    inc="cYE")
                            else:
                                P.op("vector", lambda e, S, j=j, ntc=ntc: e.tensor_tensor(
                                    y_acc[:, j, ntc:ntc + 256], psy[:, j * 256:j * 256 + 256],
                                    y_acc[:, j, ntc:ntc + 256], op=AL.add),
                                    waits=[y_pair], inc="cYE")
                    if half == 0:
                        slot_free[("w1", sl)] = slot_free[("w1", sl)] + [("cH", P.cnt["cH"])]
                        slot_free[("w2", sl)] = slot_free[("w2", sl)] + [("cY", P.cnt["cY"])]
                    else:
                        slot_free[("w1", sl)] = [("cH", P.cnt["cH"])]
                        slot_free[("w2", sl)] = [("cY", P.cnt["cY"])]
                yacc_rdy = P.cur("cYE")
                for rc in range(6):
                    sl2 = rc % 2
                    for j in range(8):
                        pb = (rc * 8 + j) % 2
                        P.op("tensor", lambda e, S, rc=rc, j=j, pb=pb: e.matmul(
                            pst[pb][:, 0:128], y_acc[:, j, rc * 128:rc * 128 + 128],
                            idn[:], is_transpose=True),
                            waits=([yacc_rdy] if (rc == 0 and j == 0) else []) +
                                  [("cYS", max(0, P.cnt["cYS"] - 1))],
                            inc="cYT")
                        P.op("vector", lambda e, S, rc=rc, j=j, pb=pb, sl2=sl2: e.tensor_scalar(
                            out=ystage[:, sl2, j * 128:j * 128 + 128],
                            in0=pst[pb][:, 0:128], scalar1=gcol[:, rc:rc + 1],
                            scalar2=None, op0=AL.mult),
                            waits=[P.cur("cYT"), ("cXF", None)] +
                              ([("sAD", max(0, P.cnt["sAD"] - 16))] if j == 0 else []),
                            inc="cYS")
                    P.op("gpsimd", lambda e, S, rc=rc, sl2=sl2: e.indirect_dma_start(
                        out=out, out_offset=bass.IndirectOffsetOnAxis(ap=idtu[:, rc:rc + 1], axis=0),
                        in_=ystage[:, sl2, :], in_offset=None, compute_op=AL.add,
                        bounds_check=S["bndo"], oob_is_err=False),
                        waits=[("cYS", None), ("sAD", None), PH[2]],
                        inc="sAD", n=16)
                    sad_col[rc] = P.cur("sAD")
                ch_free[0] = None
                for rc in range(6):
                    ch_free[rc] = ("cXT", P.cnt["cXT"])

        P.op("gpsimd", lambda e, S: None,
             waits=[("sAD", P.cnt["sAD"])] + PH[1:])
        moe.close()

        with nc.Block() as blk:
            for eng_name in ENGINES:
                dec = getattr(blk, eng_name)

                def make(eng_name=eng_name):
                    def body(e):
                        for en, f in P.steps:
                            if en == eng_name:
                                f(e, S)
                    return body
                dec(make())
    return nc


def host_prep(inputs):
    Wq = np.asarray(inputs["L0_attn_Wq"]); bq = np.asarray(inputs["L0_attn_bq"])
    Wk = np.asarray(inputs["L0_attn_Wk"]); bk = np.asarray(inputs["L0_attn_bk"])
    Wv = np.asarray(inputs["L0_attn_Wv"]); bv = np.asarray(inputs["L0_attn_bv"])
    Wo = np.asarray(inputs["L0_attn_Wo"]); bo = np.asarray(inputs["L0_attn_bo"])
    gw = np.asarray(inputs["L0_ffn_gate_w"]); gb = np.asarray(inputs["L0_ffn_gate_b"])
    w1 = np.asarray(inputs["L0_ffn_w1"]); b1 = np.asarray(inputs["L0_ffn_b1"])
    w2 = np.asarray(inputs["L0_ffn_w2"]); b2 = np.asarray(inputs["L0_ffn_b2"])
    x = np.asarray(inputs["L0_attn_x"])

    p = np.arange(128)
    permq = np.zeros(1024, np.int64)
    for c in range(4):
        h = 4 * c + p // 32
        i = p % 32
        permq[128 * c + p] = 64 * h + 2 * i
        permq[128 * (4 + c) + p] = 64 * h + 2 * i + 1
    g_ = p // 32
    i_ = p % 32
    permk = np.concatenate([64 * g_ + 2 * i_, 64 * g_ + 2 * i_ + 1])
    permo = np.zeros(1024, np.int64)
    for c in range(4):
        h = 4 * c + p // 32
        d_ = p % 32
        permo[128 * c + p] = 64 * h + d_
        permo[128 * (4 + c) + p] = 64 * h + 32 + d_

    Wqp = np.ascontiguousarray(Wq[:, permq]); bqp = bq[permq]
    Wkp = np.ascontiguousarray(Wk[:, permk]); bkp = bk[permk]
    Wop = np.ascontiguousarray(Wo[permo, :])

    bq_sb = np.ascontiguousarray(bqp.reshape(8, 128).T).astype(np.float32)
    bkv = np.stack([bkp[:128], bkp[128:], bv[:128], bv[128:]], axis=1).astype(np.float32)
    bo_sb = np.ascontiguousarray(bo.reshape(8, 128).T).astype(np.float32)
    gw_sb = np.ascontiguousarray(gw.T.reshape(8, 128, 8).transpose(1, 0, 2)).astype(np.float32)
    gb_sb = np.ascontiguousarray(gb.reshape(8, 1)).astype(np.float32)
    b1_sb = np.ascontiguousarray(b1.reshape(8, 32, 128).transpose(2, 0, 1)).astype(np.float32)
    b2_sb = np.ascontiguousarray(b2.reshape(8, 8, 128).transpose(2, 0, 1)).astype(np.float32)

    pos = np.arange(T, dtype=np.float64)
    freq = np.exp(-(2.0 * np.arange(32, dtype=np.float64) / 64.0) * np.log(10000.0))
    fr = np.tile(freq, 4)
    ang = fr[:, None] * pos[None, :]
    cosd = np.cos(ang).astype(np.float32)
    sind = np.sin(ang).astype(np.float32)
    import ml_dtypes
    mskd = np.zeros((128, 4, 512), np.float32)
    ar = np.arange(512)
    for j in range(4):
        for pp in range(128):
            mskd[pp, j, :] = np.where(pp + 128 * j <= ar, 0.0, -1e9)
    mskd = mskd.astype(ml_dtypes.bfloat16)
    idnd = np.eye(128, dtype=np.float32)
    selmd = np.zeros((128, 128), np.float32)
    for pp in range(128):
        selmd[32 * (pp // 32), pp] = 1.0

    tkdd = (np.arange(32)[None, :] * 128 + np.arange(128)[:, None]).astype(np.float32)
    shared = dict(wq=Wqp, wk=Wkp, wv=np.ascontiguousarray(Wv), wo=Wop, tkdd=tkdd,
                  bqd=bq_sb, bkvd=np.ascontiguousarray(bkv), bod=bo_sb,
                  gwd=gw_sb, gbd=gb_sb, b1d=b1_sb, b2d=b2_sb, w1d=w1, w2d=w2,
                  cosd=cosd, sind=sind, mskd=mskd, idnd=idnd, selmd=selmd)
    xs = [np.ascontiguousarray(x[2 * c:2 * c + 2].reshape(NSEQ * T, D).T)
          for c in range(NCORES)]
    return shared, xs


def kernel(**inputs):
    shared, xs = host_prep(inputs)
    nc = build_nc()
    in_maps = [dict(shared, xin=xs[c]) for c in range(NCORES)]
    res = run_bass_kernel_spmd(nc, in_maps, core_ids=list(range(NCORES)))
    outs = [np.asarray(res.results[c]["out"]).reshape(NSEQ, T, D)
            for c in range(NCORES)]
    return np.concatenate(outs, axis=0)


# version 10
# speedup vs baseline: 1.0399x; 1.0399x over previous
import numpy as np
from contextlib import ExitStack
import concourse.bass as bass
import concourse.mybir as mybir
from concourse.bass_utils import run_bass_kernel_spmd

F32 = mybir.dt.float32
AF = mybir.ActivationFunctionType
AL = mybir.AluOpType

NCORES = 8
NSEQ, T, D = 2, 2048, 1024
NQB, QBT = 4, 1024
E, H = 8, 4096
NTH, THT = 4, 768

ENGINES = ("sync", "tensor", "vector", "scalar", "gpsimd")


class Prog:
    def __init__(self):
        self.steps = []
        self.cnt = {}
        self._waited = {e: {} for e in ENGINES}

    def sem(self, name):
        self.cnt.setdefault(name, 0)

    def op(self, eng, build, waits=(), inc=None, n=1):
        wl = []
        for s, v in waits:
            val = self.cnt[s] if v is None else v
            if val <= 0:
                continue
            if self._waited[eng].get(s, -1) >= val:
                continue
            self._waited[eng][s] = val
            wl.append((s, val))
        if inc is not None:
            self.cnt[inc] += n

        def f(e, S, wl=wl, build=build, inc=inc, n=n):
            for s, v in wl:
                e.wait_ge(S[s], v)
            ins = build(e, S)
            if inc is not None:
                assert ins is not None, "inc on op that returned no instruction"
                ins.then_inc(S[inc], n)
        self.steps.append((eng, f))

    def cur(self, sem):
        return (sem, self.cnt[sem])


def build_nc():
    nc = bass.Bass("TRN2", target_bir_lowering=False)
    P = Prog()

    def din(name, shape):
        return nc.dram_tensor(name, shape, F32, kind="ExternalInput").ap()
    xin = din("xin", [D, NQB * QBT])
    wq, wk, wv, wo = din("wq", [D, D]), din("wk", [D, 256]), din("wv", [D, 256]), din("wo", [D, D])
    bqd, bkvd, bod = din("bqd", [128, 8]), din("bkvd", [128, 4]), din("bod", [128, 8])
    gwd, gbd = din("gwd", [128, 8, 8]), din("gbd", [8, 1])
    b1d, b2d = din("b1d", [128, 8, 32]), din("b2d", [128, 8, 8])
    w1d, w2d = din("w1d", [E, D, H]), din("w2d", [E, H, D])
    cosd, sind = din("cosd", [128, T]), din("sind", [128, T])
    mskd = nc.dram_tensor("mskd", [128, 4, 512], mybir.dt.bfloat16, kind="ExternalInput").ap()
    idnd, selmd = din("idnd", [128, 128]), din("selmd", [128, 128])
    tkdd = din("tkdd", [128, 32])
    BF16 = mybir.dt.bfloat16
    U16 = mybir.dt.uint16
    U32 = mybir.dt.uint32
    RW = 2080
    CAP = 1536
    out = nc.dram_tensor("out", [NQB * QBT, D], F32, kind="ExternalOutput").ap()
    w1hd = nc.dram_tensor("w1hd", [E, 8, 128, 8, 512], BF16, kind="Internal").ap()
    w1ld = nc.dram_tensor("w1ld", [E, 8, 128, 8, 512], BF16, kind="Internal").ap()
    w2hd = nc.dram_tensor("w2hd", [E, 8, 128, 4, 1024], BF16, kind="Internal").ap()
    w2ld = nc.dram_tensor("w2ld", [E, 8, 128, 4, 1024], BF16, kind="Internal").ap()
    xbuf = nc.dram_tensor("xbuf", [E, CAP, RW], U16, kind="Internal").ap()

    xin_r = xin.rearrange("(kc p) n -> p kc n", p=128)
    xbuf_z = xbuf.rearrange("e c w -> (e c w)")
    xbuf_g = xbuf.rearrange("e (rc p) w -> e p rc w", p=128)
    wq_r = wq.rearrange("(kc p) m -> p kc m", p=128)
    wk_r = wk.rearrange("(kc p) m -> p kc m", p=128)
    wv_r = wv.rearrange("(kc p) m -> p kc m", p=128)
    wo_r = wo.rearrange("(kc p) m -> p kc m", p=128)
    w1_r = w1d.rearrange("e (kc p) m -> e p kc m", p=128)
    w2_r = w2d.rearrange("e (mc p) m -> e p mc m", p=128)

    with ExitStack() as top:
        sb = lambda name, shape: top.enter_context(nc.sbuf_tensor(name, shape, F32))
        mskb = top.enter_context(nc.sbuf_tensor("mskb", [128, 4, 512], mybir.dt.bfloat16))
        idn = sb("idn", [128, 128])
        selm = sb("selm", [128, 128])
        onc = sb("onc", [128, 1])
        onr = sb("onr", [1, 128])
        epsb = sb("epsb", [128, 1])
        bq_sb = sb("bq_sb", [128, 8])
        bkv_sb = sb("bkv_sb", [128, 4])
        bo_sb = sb("bo_sb", [128, 8])
        gw_sb = sb("gw_sb", [128, 8, 8])
        gb_sb = sb("gb_sb", [8, 1])
        b1_sb = sb("b1_sb", [128, 8, 32])
        b2_sb = sb("b2_sb", [128, 8, 8])

        S = {}
        semnames = ["sC", "sX", "sWQ", "sWKV", "sWO", "sCS", "sW1", "sW2", "sXN",
                    "sSP", "sXO", "sGB", "sGR",
                    "cSQ", "cST", "cBC", "cXN", "cQP", "cQE", "cKV", "cKE",
                    "cVT", "cVE", "cKR", "cSC", "cMK", "cEX", "cAV", "cAE",
                    "cSE", "cRC", "cNX", "cNM", "cWO", "cG", "cGT", "phA",
                    "cH", "cHE", "cY", "cYE", "cSTG", "cVv", "cRV", "cSV", "cIN", "cWS", "cHL", "sWB",
                    "sZB", "cXT", "cXE", "cRT", "cRH", "cRL", "sAD", "sGA", "cYT", "cYS", "cXF", "cXG"] + [f"sA{r}" for r in range(16)]
        for name in semnames:
            S[name] = top.enter_context(nc.semaphore(name))
            P.sem(name)

        def _setup_regs(e, S):
            S["bndx"] = []
            for i in range(E):
                r = e.alloc_register(f"bnd{i}")
                e.reg_mov(r, (i + 1) * CAP - 1)
                S["bndx"].append(r)
            r = e.alloc_register("bndo")
            e.reg_mov(r, NQB * QBT - 1)
            S["bndo"] = r
        P.op("gpsimd", lambda e, S: _setup_regs(e, S) or None)

        def dma(qeng, dst, src, inc, waits=()):
            P.op(qeng, lambda e, S, d=dst, s=src: e.dma_start(out=d, in_=s),
                 waits=waits, inc=inc, n=16)

        for dst, src in ((mskb[:], mskd), (idn[:], idnd), (selm[:], selmd),
                         (bq_sb[:], bqd), (bkv_sb[:], bkvd), (bo_sb[:], bod),
                         (gw_sb[:], gwd), (gb_sb[:], gbd), (b1_sb[:], b1d),
                         (b2_sb[:], b2d)):
            dma("sync", dst, src, "sC")
        P.op("vector", lambda e, S: e.memset(onc[:], 1.0), inc="cIN")
        P.op("vector", lambda e, S: e.memset(epsb[:], 1e-5), inc="cIN")
        P.op("vector", lambda e, S: e.memset(onr[:], 1.0), inc="cIN")
        CW = [P.cur("sC"), P.cur("cIN")]

        att = ExitStack()
        asb = lambda name, shape: att.enter_context(nc.sbuf_tensor(name, shape, F32))
        xq = asb("xq", [128, 8, QBT])
        qa = asb("qa", [128, 8, QBT])
        ksb = asb("ksb", [128, 2, T])
        krep = asb("krep", [128, 2, T])
        vsb = asb("vsb", [128, 2, 2, 512])
        vtm = asb("vtm", [128, 4, 16, 66])
        expb = asb("expb", [128, 2, 4, 512])
        csb = asb("csb", [128, 2, 512])
        wqs = asb("wqs", [128, 2, 8, 128])
        wkvs = asb("wkvs", [128, 2, 4, 128])
        tmp3 = asb("tmp3", [128, 3, 512])
        xnt = asb("xnt", [128, 2, 512])
        meanb = asb("meanb", [128, 512])
        rstdb = asb("rstdb", [128, 512])
        sume = asb("sume", [128, QBT])
        reci = asb("reci", [128, 1, QBT])
        recb = asb("recb", [128, 512])
        lgq = asb("lgq", [8, QBT])
        gsct = asb("gsct", [128, 64])
        gsc2 = asb("gsc2", [128, 2, 64])
        gsm = asb("gsm", [128, 4, 8])
        xts = asb("xts", [128, 2, 1024])
        rsg = att.enter_context(nc.sbuf_tensor("rsg", [128, 2, 2080], mybir.dt.uint16))
        sidxT = asb("sidxT", [128, 8, 8])
        idxu8 = att.enter_context(nc.sbuf_tensor("idxu8", [128, 2, 8], mybir.dt.uint32))
        carry = asb("carry", [8, 1])
        ebias = asb("ebias", [128, 8])
        tkd = asb("tkd", [128, 32])
        ps = [att.enter_context(nc.psum_tensor(f"ps{i}", [128, 512], F32)) for i in range(8)]
        statm = reci[0:1, 0, 0:512]
        statm2 = reci[0:1, 0, 512:1024]
        statv = reci[0:1, 0, 512:1024]

        P.op("vector", lambda e, S: e.memset(sume[:], 1.0), inc="cIN")
        P.op("vector", lambda e, S: e.memset(carry[:], 0.0), inc="cIN")
        P.op("gpsimd", lambda e, S: e.iota(ebias[:], [[CAP, 8]], base=0,
             channel_multiplier=0, allow_small_or_imprecise_dtypes=True), inc="cWS")
        P.op("vector", lambda e, S: e.memset(xts[:], 0.0), inc="cIN")
        P.op("vector", lambda e, S: e.memset(rsg[:], 0), inc="cIN")
        dma("sync", tkd[:], tkdd, "sC")
        CW[1] = P.cur("cIN")
        ztot = E * CAP * RW
        zchunk = 128 * 4096
        xts_z16 = xts[:].bitcast(mybir.dt.uint16)
        zoff = 0
        while zoff < ztot:
            n_el = min(zchunk, ztot - zoff)
            npart = n_el // 4096
            P.op("scalar", lambda e, S, zoff=zoff, npart=npart: e.dma_start(
                out=xbuf_z[zoff:zoff + npart * 4096].rearrange("(p n) -> p n", n=4096),
                in_=xts_z16[0:npart, :, :]),
                waits=[CW[1]], inc="sZB", n=16)
            zoff += npart * 4096
        ZB = P.cur("sZB")

        slot_free = {}

        def LN_stats(xsrc_chunk, first_waits, snt):
            st_base = P.cnt["cST"]
            sq_base = P.cnt["cSQ"]
            for kc in range(8):
                sl = kc % 2
                P.op("scalar", lambda e, S, kc=kc, sl=sl, snt=snt: e.activation(
                    vsb[:, sl, snt, :], xsrc_chunk(kc), AF.Square),
                    waits=(first_waits if kc == 0 else []) +
                          [("cQE", None), ("cKE", None)] +
                          ([("cST", st_base)] if kc == 0 else
                           [("cST", st_base + kc - 1)] if kc >= 2 else []),
                    inc="cSQ")
                P.op("tensor", lambda e, S, kc=kc: e.matmul(
                    ps[6][0:1, :], onc[:], xsrc_chunk(kc),
                    start=(kc == 0), stop=(kc == 7)),
                    waits=(first_waits if kc == 0 else []) +
                          ([("cSQ", None), ("cSV", None), ("cNM", None), CW[1]]
                           if kc == 0 else []))
                P.op("tensor", lambda e, S, kc=kc, sl=sl, snt=snt: e.matmul(
                    ps[7][0:1, :], onc[:], vsb[:, sl, snt, :],
                    start=(kc == 0), stop=(kc == 7)),
                    waits=[("cSQ", sq_base + kc + 1)], inc="cST")
            smm = ("cST", st_base + 8)
            P.op("vector", lambda e, S: e.tensor_scalar(
                out=statm, in0=ps[6][0:1, :], scalar1=1.0 / D, scalar2=None,
                op0=AL.mult),
                waits=[smm, ("cQP", None), ("cKV", None), ("cBC", None),
                       ("cXN", None)], inc="cSV")
            P.op("scalar", lambda e, S: e.activation(statm2, statm, AF.Square),
                 waits=[("cSV", None)], inc="cSQ")
            P.op("vector", lambda e, S: e.scalar_tensor_tensor(
                out=statv, in0=ps[7][0:1, :], scalar=1.0 / D, in1=statm2,
                op0=AL.mult, op1=AL.subtract), waits=[("cSQ", None)], inc="cSV")
            P.op("tensor", lambda e, S: e.matmul(
                ps[6][:, :], onr[:], statm, start=True, stop=True),
                waits=[("cSV", None), ("cSQ", None)])
            P.op("tensor", lambda e, S: e.matmul(
                ps[7][:, :], onr[:], statv, start=True, stop=True), inc="cBC")
            bmm = P.cur("cBC")
            P.op("scalar", lambda e, S: e.copy(meanb[:], ps[6][:, :]),
                 waits=[bmm, ("cXN", None)], inc="cSQ")
            P.op("scalar", lambda e, S: e.activation(
                rstdb[:], ps[7][:, :], AF.Sqrt, bias=epsb[:, 0:1]),
                waits=[bmm, ("cXN", None)], inc="cSQ")
            P.op("vector", lambda e, S: e.reciprocal(rstdb[:], rstdb[:]),
                 waits=[("cSQ", None)], inc="cNM")
            return [P.cur("cSQ"), P.cur("cNM")]

        def xn_make(xsrc, dst, ln_rdy, extra, snt):
            P.op("vector", lambda e, S: e.tensor_tensor(
                tmp3[:, 2, :], xsrc, meanb[:], op=AL.subtract),
                waits=ln_rdy + extra, inc="cXN")
            P.op("vector", lambda e, S: e.tensor_tensor(
                dst, tmp3[:, 2, :], rstdb[:], op=AL.mult), inc="cXN")
            return P.cur("cXN")

        def rope(eS, oS, waits0):
            P.op("vector", lambda e, S: e.tensor_tensor(
                tmp3[:, 0, :], eS, csb[:, 1, :], op=AL.mult), waits=waits0)
            P.op("vector", lambda e, S: e.tensor_tensor(
                tmp3[:, 1, :], eS, csb[:, 0, :], op=AL.mult))
            P.op("vector", lambda e, S: e.tensor_tensor(
                tmp3[:, 2, :], oS, csb[:, 1, :], op=AL.mult))
            P.op("vector", lambda e, S: e.tensor_tensor(
                eS, tmp3[:, 1, :], tmp3[:, 2, :], op=AL.subtract))
            P.op("vector", lambda e, S: e.tensor_tensor(
                tmp3[:, 1, :], oS, csb[:, 0, :], op=AL.mult))

            def fin(e, S):
                return e.tensor_tensor(oS, tmp3[:, 0, :], tmp3[:, 1, :], op=AL.add)
            return fin

        for qb in range(NQB):
            seq, qbl = qb // 2, qb % 2
            qc0 = qb * QBT
            sc0 = qbl * QBT

            P.op("gpsimd", lambda e, S, c=qc0: e.dma_start(
                out=xq[:], in_=xin_r[:, :, c:c + QBT]),
                waits=[("cXT", None), ("cST", None), ("cXN", None)],
                inc="sX", n=16)
            xrdy = P.cur("sX")

            ln_rdy_nt = {}
            for nt in range(2):
                ntc = nt * 512
                dma("sync", csb[:, 0, :], cosd[:, sc0 + ntc:sc0 + ntc + 512], "sCS",
                    waits=[("cQE", None), ("cKE", None)])
                dma("sync", csb[:, 1, :], sind[:, sc0 + ntc:sc0 + ntc + 512], "sCS")
                cs_rdy = P.cur("sCS")

                if nt == 0:
                    ln_rdy_nt[0] = LN_stats(
                        lambda kc, ntc=ntc: xq[:, kc, ntc:ntc + 512], [xrdy], 0)
                ln_rdy = ln_rdy_nt[nt]

                for kc in range(8):
                    sl = kc % 2
                    wwl = [slot_free[("wq", sl)]] if ("wq", sl) in slot_free else []
                    dma("sync", wqs[:, sl, :, :], wq_r[:, kc, :], "sWQ", waits=wwl)
                    wrdy = P.cur("sWQ")
                    xn_rdy = xn_make(xq[:, kc, ntc:ntc + 512], xnt[:, sl, :], ln_rdy,
                                     [("cQP", P.cnt["cQP"] - 1 if kc >= 2 else 0)], nt)
                    for half in range(2):
                        for c in range(4):
                            last = (half == 1 and c == 3)
                            P.op("tensor", lambda e, S, kc=kc, sl=sl, half=half, c=c: e.matmul(
                                ps[2 * c + half][:, :], wqs[:, sl, 4 * half + c, :],
                                xnt[:, sl, :], start=(kc == 0), stop=(kc == 7)),
                                waits=[wrdy, xn_rdy] +
                                      ([("cQE", None), ("cKE", None), ("cVE", None),
                                        ("cVT", None), ("cVv", None)] if kc == 0 else []),
                                inc=("cQP" if last else None))
                    slot_free[("wq", sl)] = P.cur("cQP")
                qp_mm = P.cur("cQP")
                for c in range(4):
                    P.op("vector", lambda e, S, c=c, ntc=ntc: e.tensor_scalar(
                        out=qa[:, c, ntc:ntc + 512], in0=ps[2 * c][:, :],
                        scalar1=bq_sb[:, c:c + 1], scalar2=None, op0=AL.add),
                        waits=[qp_mm, cs_rdy, ("sSP", None)], inc="cQE")
                    P.op("vector", lambda e, S, c=c, ntc=ntc: e.tensor_scalar(
                        out=qa[:, 4 + c, ntc:ntc + 512], in0=ps[2 * c + 1][:, :],
                        scalar1=bq_sb[:, 4 + c:5 + c], scalar2=None, op0=AL.add),
                        inc="cQE")
                    fin = rope(qa[:, c, ntc:ntc + 512], qa[:, 4 + c, ntc:ntc + 512], [])
                    P.op("vector", fin, inc="cQE")
                q_rope = P.cur("cQE")

                for kc in range(8):
                    sl = kc % 2
                    wwl = [slot_free[("wkv", sl)]] if ("wkv", sl) in slot_free else []
                    dma("sync", wkvs[:, sl, 0:2, :], wk_r[:, kc, :], "sWKV", waits=wwl)
                    dma("sync", wkvs[:, sl, 2:4, :], wv_r[:, kc, :], "sWKV")
                    wrdy = P.cur("sWKV")
                    xn_rdy = xn_make(xq[:, kc, ntc:ntc + 512], xnt[:, sl, :], ln_rdy,
                                     [q_rope, ("cKV", P.cnt["cKV"] - 1 if kc >= 2 else 0)], nt)
                    for m in range(4):
                        P.op("tensor", lambda e, S, kc=kc, sl=sl, m=m: e.matmul(
                            ps[m][:, :], wkvs[:, sl, m, :], xnt[:, sl, :],
                            start=(kc == 0), stop=(kc == 7)),
                            waits=[wrdy, xn_rdy, qp_mm] +
                                  ([("cKE", None), ("cVE", None)] if kc == 0 else []),
                            inc=("cKV" if m == 3 else None))
                    slot_free[("wkv", sl)] = P.cur("cKV")
                kv_mm = P.cur("cKV")
                kcol = sc0 + ntc
                for half in range(2):
                    P.op("vector", lambda e, S, half=half, kcol=kcol: e.tensor_scalar(
                        out=ksb[:, half, kcol:kcol + 512], in0=ps[half][:, :],
                        scalar1=bkv_sb[:, half:half + 1], scalar2=None, op0=AL.add),
                        waits=[kv_mm, cs_rdy, ("cKR", None)], inc="cKE")
                fin = rope(ksb[:, 0, kcol:kcol + 512], ksb[:, 1, kcol:kcol + 512],
                           [("cKE", None)])
                P.op("vector", fin, inc="cKE")
                b = nt % 2
                for ch in range(2):
                    P.op("vector", lambda e, S, ch=ch, b=b: e.tensor_scalar(
                        out=vsb[:, ch, b, :], in0=ps[2 + ch][:, :],
                        scalar1=bkv_sb[:, 2 + ch:3 + ch], scalar2=None, op0=AL.add),
                        waits=[kv_mm, ("cVT", None)], inc="cVE")
                v_ev = P.cur("cVE")
                if nt == 0:
                    ln_rdy_nt[1] = LN_stats(
                        lambda kc: xq[:, kc, 512:1024], [xrdy], 1)
                for ch in range(2):
                    for sub in range(4):
                        pb = 4 + ((ch * 4 + sub) % 2)
                        kt = (sc0 + ntc + sub * 128) // 128
                        P.op("tensor", lambda e, S, ch=ch, sub=sub, pb=pb, b=b: e.matmul(
                            ps[pb][:, 0:128], vsb[:, ch, b, sub * 128:sub * 128 + 128],
                            idn[:], is_transpose=True),
                            waits=[v_ev, CW[0], ("cVv", max(0, P.cnt["cVv"] - 1))],
                            inc="cVT")
                        vmm = P.cur("cVT")
                        P.op("scalar", lambda e, S, ch=ch, kt=kt, pb=pb: e.copy(
                            vtm[:, 2 * ch, kt, 0:64], ps[pb][:, 0:64]), waits=[vmm])
                        P.op("scalar", lambda e, S, ch=ch, kt=kt, pb=pb: e.copy(
                            vtm[:, 2 * ch + 1, kt, 0:64], ps[pb][:, 64:128]),
                            inc="cVv")
            kt0 = sc0 // 128
            P.op("vector", lambda e, S, kt0=kt0: e.memset(
                vtm[:, :, kt0:kt0 + 8, 64:65], 1.0), waits=[("cVT", None)], inc="cVE")
            vtm_rdy = [P.cur("cVT"), P.cur("cVv"), P.cur("cVE")]

            n_keys = (qbl + 1) * QBT
            for c in range(4):
                for j in range(4):
                    for half in range(2):
                        P.op("scalar", lambda e, S, j=j, half=half, c=c, n_keys=n_keys: e.copy(
                            krep[32 * j:32 * j + 32, half, 0:n_keys],
                            ksb[32 * c:32 * c + 32, half, 0:n_keys]),
                            waits=[("cKE", None), ("cSC", None)],
                            inc=("cKR" if (j == 3 and half == 1) else None))
                krep_rdy = P.cur("cKR")
                for qt in range(2):
                    qtc = qt * 512
                    n_kt = (sc0 + qtc + 512) // 128
                    for kt in range(n_kt):
                        for j in range(4):
                            for half in range(2):
                                last = (j == 3 and half == 1)
                                P.op("tensor", lambda e, S, j=j, half=half, kt=kt, c=c, qtc=qtc: e.matmul(
                                    ps[j][:, :],
                                    krep[32 * j:32 * j + 32, half, kt * 128:kt * 128 + 128],
                                    qa[32 * j:32 * j + 32, (c if half == 0 else 4 + c),
                                       qtc:qtc + 512],
                                    start=(half == 0), stop=(half == 1),
                                    tile_position=(32 * j, 0)),
                                    waits=([krep_rdy, q_rope, ("cEX", None),
                                            ("cNX", None)] if (j == 0 and half == 0) else []),
                                    inc=("cSC" if last else None))
                        sc_mm = P.cur("cSC")
                        joff = kt - (n_kt - 4)
                        if joff >= 0:
                            for j in range(4):
                                P.op("vector", lambda e, S, j=j, joff=joff: e.tensor_tensor(
                                    ps[j][:, :], ps[j][:, :], mskb[:, joff, :], op=AL.add),
                                    waits=([sc_mm, CW[0]] if j == 0 else []),
                                    inc=("cMK" if j == 3 else None))
                            exw = [P.cur("cMK")]
                        else:
                            exw = [sc_mm]
                        for j in range(4):
                            P.op("scalar", lambda e, S, j=j, kt=kt: e.activation(
                                expb[:, kt % 2, j, :], ps[j][:, :], AF.Exp, scale=0.125),
                                waits=(exw + [("cAV", max(0, P.cnt["cAV"] - 1))]) if j == 0 else [],
                                inc=("cEX" if j == 3 else None))
                        ex_rdy = P.cur("cEX")
                        for j in range(4):
                            P.op("tensor", lambda e, S, j=j, kt=kt, c=c, n_kt=n_kt: e.matmul(
                                ps[4 + j][0:65, :], vtm[:, c, kt, 0:65],
                                expb[:, kt % 2, j, :],
                                start=(kt == 0), stop=(kt == n_kt - 1)),
                                waits=([ex_rdy] + (vtm_rdy + [("cAE", None)] if kt == 0 else [])) if j == 0 else [],
                                inc=("cAV" if j == 3 else None))
                    av_done = P.cur("cAV")
                    for j in range(4):
                        P.op("scalar", lambda e, S, j=j, qtc=qtc: e.copy(
                            sume[32 * j:32 * j + 1, qtc:qtc + 512],
                            ps[4 + j][64:65, :]),
                            waits=([av_done, ("cRV", None), ("cNX", None)] if j == 0 else []),
                            inc=("cSE" if j == 3 else None))
                        P.op("scalar", lambda e, S, j=j, c=c, qtc=qtc: e.copy(
                            qa[32 * j:32 * j + 32, c, qtc:qtc + 512], ps[4 + j][0:32, :]))
                        P.op("scalar", lambda e, S, j=j, c=c, qtc=qtc: e.copy(
                            qa[32 * j:32 * j + 32, 4 + c, qtc:qtc + 512],
                            ps[4 + j][32:64, :]),
                            inc=("cAE" if j == 3 else None))
                se_rdy = P.cur("cSE")
                ae_rdy = P.cur("cAE")
                P.op("scalar", lambda e, S: e.activation(
                    reci[:, 0, :], sume[:, :], AF.Ln),
                    waits=[se_rdy, ("cNM", None)], inc="cRC")
                P.op("scalar", lambda e, S: e.activation(
                    reci[:, 0, :], reci[:, 0, :], AF.Exp, scale=-1.0), inc="cRC")
                nsc = xts[:, 0, 0:QBT]
                P.op("vector", lambda e, S, nsc=nsc: e.tensor_tensor(
                    nsc, sume[:, :], reci[:, 0, :], op=AL.mult),
                    waits=[("cRC", None), ("sXO", None), ("sZB", None)], inc="cRV")
                P.op("vector", lambda e, S, nsc=nsc: e.tensor_scalar(
                    out=nsc, in0=nsc, scalar1=-1.0, scalar2=2.0,
                    op0=AL.mult, op1=AL.add), inc="cRV")
                P.op("vector", lambda e, S, nsc=nsc: e.tensor_tensor(
                    sume[:, :], reci[:, 0, :], nsc, op=AL.mult), inc="cRV")
                rc_rdy = P.cur("cRV")
                for qt in range(2):
                    qtc = qt * 512
                    P.op("tensor", lambda e, S, qtc=qtc: e.matmul(
                        ps[0][:, :], selm[:], sume[:, qtc:qtc + 512],
                        start=True, stop=True),
                        waits=[rc_rdy, CW[0], ("cEX", None), ("cNX", None)], inc="cSC")
                    P.op("vector", lambda e, S: e.tensor_copy(recb[:], ps[0][:, :]),
                         waits=[("cSC", None)], inc="cNX")
                    for u in range(2):
                        P.op("vector", lambda e, S, c=c, u=u, qtc=qtc: e.tensor_tensor(
                            qa[:, c + 4 * u, qtc:qtc + 512],
                            qa[:, c + 4 * u, qtc:qtc + 512], recb[:], op=AL.mult),
                            waits=([ae_rdy] if u == 0 else []),
                            inc=("cNM" if u == 1 else None))
            nm_rdy = P.cur("cNM")

            for m in range(8):
                sl = m % 2
                wwl = [slot_free[("wo", sl)]] if ("wo", sl) in slot_free else []
                if ("wq", sl) in slot_free:
                    wwl.append(slot_free[("wq", sl)])
                dma("sync", wqs[:, sl, :, :], wo_r[:, :, m * 128:m * 128 + 128],
                    "sWO", waits=wwl)
                wrdy = P.cur("sWO")
                for nt in range(2):
                    pb = 4 + nt
                    for kc in range(8):
                        P.op("tensor", lambda e, S, m=m, sl=sl, nt=nt, kc=kc, pb=pb: e.matmul(
                            ps[pb][:, :], wqs[:, sl, kc, :],
                            qa[:, kc, nt * 512:nt * 512 + 512],
                            start=(kc == 0), stop=(kc == 7)),
                            waits=([wrdy, nm_rdy, ("cWO", max(0, P.cnt["cWO"] - 1))]
                                   if kc == 0 else []),
                            inc=("cG" if kc == 7 else None))
                    womm = P.cur("cG")
                    P.op("vector", lambda e, S, m=m, nt=nt, pb=pb: e.scalar_tensor_tensor(
                        out=xq[:, m, nt * 512:nt * 512 + 512], in0=ps[pb][:, :],
                        scalar=bo_sb[:, m:m + 1],
                        in1=xq[:, m, nt * 512:nt * 512 + 512],
                        op0=AL.add, op1=AL.add), waits=[womm], inc="cWO")
                slot_free[("wo", sl)] = P.cur("cWO")
                slot_free[("wq", sl)] = P.cur("cG")
            xp_rdy = P.cur("cWO")

            ln2_rdy_nt = {0: LN_stats(lambda kc: xq[:, kc, 0:512], [xp_rdy], 0)}
            for nt in range(2):
                ntc = nt * 512
                ln_rdy = ln2_rdy_nt[nt]
                for kc in range(8):
                    xn_rdy = xn_make(xq[:, kc, ntc:ntc + 512], qa[:, kc, ntc:ntc + 512],
                                     ln_rdy, [("cG", None), ("sSP", None)], nt)
                    P.op("tensor", lambda e, S, kc=kc, ntc=ntc: e.matmul(
                        ps[5][0:8, :], gw_sb[:, kc, :], qa[:, kc, ntc:ntc + 512],
                        start=(kc == 0), stop=(kc == 7)),
                        waits=[xn_rdy] + ([CW[0], xp_rdy, ("cGT", None)] if kc == 0 else []),
                        inc=("cG" if kc == 7 else None))
                if nt == 0:
                    ln2_rdy_nt[1] = LN_stats(
                        lambda kc: xq[:, kc, 512:1024], [xp_rdy], 1)
                P.op("vector", lambda e, S, ntc=ntc: e.tensor_scalar(
                    out=lgq[0:8, ntc:ntc + 512], in0=ps[5][0:8, :],
                    scalar1=gb_sb[0:8, 0:1], scalar2=None, op0=AL.add),
                    waits=[("cG", None), ("cGT", None)], inc="cGT")
            lg_rdy = P.cur("cGT")

            def gop(eng, fn, w=()):
                P.op(eng, fn, waits=list(w) + [("cGT", None)], inc="cGT")
            for st_ in range(8):
                gop("tensor", lambda e, S, st_=st_: e.matmul(
                    ps[4][:, 8 * st_:8 * st_ + 8], lgq[0:8, 128 * st_:128 * st_ + 128],
                    idn[0:8, 0:8], is_transpose=True), w=[lg_rdy, ("cAE", None)])
            gop("vector", lambda e, S: e.tensor_copy(gsct[:, :], ps[4][:, 0:64]))
            lgT = gsct.ap() if hasattr(gsct, "ap") else gsct
            gop("vector", lambda e, S: e.tensor_reduce(
                gsm[:, 0, :], gsct[:].rearrange("p (s e) -> p s e", e=8),
                mybir.AxisListType.X, AL.max))
            mxb = gsm[:, 0:1, :].rearrange("p a e -> p e a").broadcast_to((128, 8, 8))
            m2b = gsm[:, 1:2, :].rearrange("p a e -> p e a").broadcast_to((128, 8, 8))
            gop("vector", lambda e, S: e.tensor_tensor(
                gsc2[:, 1, :].rearrange("p (s e) -> p s e", e=8),
                gsct[:].rearrange("p (s e) -> p s e", e=8), mxb, op=AL.is_equal),
                w=[("cRH", None)])
            gop("vector", lambda e, S: e.scalar_tensor_tensor(
                out=gsc2[:, 1, :], in0=gsc2[:, 1, :], scalar=-1e30,
                in1=gsct[:], op0=AL.mult, op1=AL.add))
            gop("vector", lambda e, S: e.tensor_reduce(
                gsm[:, 1, :], gsc2[:, 1, :].rearrange("p (s e) -> p s e", e=8),
                mybir.AxisListType.X, AL.max))
            gop("vector", lambda e, S: e.tensor_tensor(
                gsc2[:, 0, :].rearrange("p (s e) -> p s e", e=8),
                gsct[:].rearrange("p (s e) -> p s e", e=8), mxb, op=AL.subtract))
            gop("scalar", lambda e, S: e.activation(
                gsc2[:, 0, :], gsc2[:, 0, :], AF.Exp))
            gop("vector", lambda e, S: e.tensor_reduce(
                gsm[:, 2, :], gsc2[:, 0, :].rearrange("p (s e) -> p s e", e=8),
                mybir.AxisListType.X, AL.add))
            gop("vector", lambda e, S: e.reciprocal(gsm[:, 3, :], gsm[:, 2, :]))
            rcb = gsm[:, 3:4, :].rearrange("p a e -> p e a").broadcast_to((128, 8, 8))
            gop("vector", lambda e, S: e.tensor_tensor(
                gsc2[:, 0, :].rearrange("p (s e) -> p s e", e=8),
                gsc2[:, 0, :].rearrange("p (s e) -> p s e", e=8), rcb, op=AL.mult))
            gop("vector", lambda e, S: e.tensor_tensor(
                gsc2[:, 1, :].rearrange("p (s e) -> p s e", e=8),
                gsct[:].rearrange("p (s e) -> p s e", e=8), m2b, op=AL.is_ge))
            gop("vector", lambda e, S: e.tensor_tensor(
                gsc2[:, 1, :], gsc2[:, 0, :], gsc2[:, 1, :], op=AL.mult))
            for st_ in range(8):
                pb = 4 + st_ // 4
                gop("tensor", lambda e, S, st_=st_, pb=pb: e.matmul(
                    ps[pb][0:8, 128 * (st_ % 4):128 * (st_ % 4) + 128],
                    gsc2[:, 1, 8 * st_:8 * st_ + 8], idn[:], is_transpose=True))
            for st_ in range(8):
                pb = 4 + st_ // 4
                gop("vector", lambda e, S, st_=st_, pb=pb: e.tensor_copy(
                    lgq[0:8, 128 * st_:128 * st_ + 128],
                    ps[pb][0:8, 128 * (st_ % 4):128 * (st_ % 4) + 128]))
            rnk = reci[0:8, 0, 0:QBT]
            rsel = xts[0:8, 1, 0:QBT]
            ones8 = xnt[0:8, :, :].rearrange("p a b -> p (a b)")
            gop("vector", lambda e, S: e.memset(ones8, 1.0), w=[("cKV", None)])
            gop("vector", lambda e, S: e.tensor_scalar(
                out=rsel, in0=lgq[0:8, :], scalar1=0.0, scalar2=None, op0=AL.is_gt),
                w=[("cRC", None), ("sXO", None)])
            gop("vector", lambda e, S: e.tensor_tensor_scan(
                rnk, ones8, rsel, carry[0:8, 0:1], op0=AL.mult, op1=AL.add))
            gop("vector", lambda e, S: e.tensor_copy(carry[0:8, 0:1], rnk[:, QBT - 1:QBT]))
            gop("vector", lambda e, S: e.scalar_tensor_tensor(
                out=rsel, in0=rsel, scalar=-4194304.0, in1=rnk, op0=AL.mult, op1=AL.add))
            gop("vector", lambda e, S: e.tensor_scalar(
                out=rsel, in0=rsel, scalar1=4194303.0, scalar2=None, op0=AL.add))
            for st_ in range(8):
                gop("tensor", lambda e, S, st_=st_: e.matmul(
                    ps[6][:, 8 * st_:8 * st_ + 8], rsel[:, 128 * st_:128 * st_ + 128],
                    idn[0:8, 0:8], is_transpose=True))
            gop("vector", lambda e, S: e.tensor_copy(sidxT[:].rearrange("p a b -> p (a b)"),
                                                     ps[6][:, 0:64]))

            sxo_tt = {}
            for tt in range(8):
                sl = tt % 2
                for kc in range(8):
                    pb = 2 + ((tt * 8 + kc) % 2)
                    P.op("tensor", lambda e, S, tt=tt, kc=kc, pb=pb: e.matmul(
                        ps[pb][:, 0:128], xq[:, kc, tt * 128:tt * 128 + 128], idn[:],
                        is_transpose=True),
                        waits=([xp_rdy, ("cEX", None)] if (tt == 0 and kc == 0) else []) +
                              [("cXE", max(0, P.cnt["cXE"] - 1))],
                        inc="cXT")
                    P.op("scalar", lambda e, S, tt=tt, kc=kc, pb=pb, sl=sl: e.copy(
                        xts[:, sl, kc * 128:kc * 128 + 128], ps[pb][:, 0:128]),
                        waits=[P.cur("cXT")] +
                              ([("cGT", None), ("cRV", None)] if (kc == 0 and tt == 0) else []) +
                              ([sxo_tt[tt - 2]] if (kc == 0 and tt - 2 in sxo_tt) else []),
                        inc="cXE")
                P.op("gpsimd", lambda e, S, tt=tt, sl=sl, c=qc0: e.dma_start(
                    out=out[c + tt * 128:c + tt * 128 + 128, :], in_=xts[:, sl, :]),
                    waits=[("cXE", None)], inc="sXO", n=16)
                sxo_tt[tt] = P.cur("sXO")

            rsg_b = rsg[:].bitcast(mybir.dt.bfloat16)
            rsg_f = rsg[:].bitcast(F32)
            rsg_free = {}
            for tt in range(8):
                sl = tt % 2
                for kc in range(8):
                    pb = (tt * 8 + kc) % 2
                    P.op("tensor", lambda e, S, tt=tt, kc=kc, pb=pb: e.matmul(
                        ps[pb][:, 0:128], qa[:, kc, tt * 128:tt * 128 + 128], idn[:],
                        is_transpose=True),
                        waits=([("cG", None), ("cNX", None), ("cXN", None),
                                ("cGT", None)] if (tt == 0 and kc == 0) else []) +
                              [("cRL", max(0, P.cnt["cRL"] - 1))],
                        inc="cRT")
                    P.op("scalar", lambda e, S, kc=kc, pb=pb, sl=sl: e.activation(
                        rsg_b[:, sl, kc * 128:kc * 128 + 128], ps[pb][:, 0:128], AF.Copy),
                        waits=[P.cur("cRT")] +
                              ([rsg_free[tt - 2]] if (kc == 0 and tt - 2 in rsg_free) else []),
                        inc="cRH")
                    P.op("vector", lambda e, S, kc=kc, pb=pb, sl=sl: e.tensor_tensor(
                        rsg_b[:, sl, 1024 + kc * 128:1024 + kc * 128 + 128], ps[pb][:, 0:128],
                        rsg_b[:, sl, kc * 128:kc * 128 + 128], op=AL.subtract),
                        waits=[P.cur("cRH")], inc="cRL")
                P.op("scalar", lambda e, S, tt=tt, sl=sl: e.copy(
                    rsg_f[:, sl, 1024:1032], gsc2[:, 1, 8 * tt:8 * tt + 8]),
                    waits=[("cGT", None)], inc="cRH")
                P.op("scalar", lambda e, S, tt=tt, sl=sl, qb=qb: e.copy(
                    rsg_f[:, sl, 1032:1033], tkd[:, qb * 8 + tt:qb * 8 + tt + 1]),
                    waits=[("sC", None)], inc="cRH")
                P.op("vector", lambda e, S, tt=tt, sl=sl: e.tensor_tensor(
                    idxu8[:, sl, :], sidxT[:, tt, :], ebias[:], op=AL.add),
                    waits=[("cGT", None), ("cWS", 1)] +
                          ([rsg_free[tt - 2]] if tt - 2 in rsg_free else []),
                    inc="cRL")
                xbuf_rows = xbuf.rearrange("e c w -> (e c) w")
                for e_x in range(E):
                    def _scat(e, S, e_x=e_x, sl=sl, xr=xbuf_rows):
                        return e.indirect_dma_start(
                            out=xr,
                            out_offset=bass.IndirectOffsetOnAxis(ap=idxu8[:, sl, e_x:e_x + 1], axis=0),
                            in_=rsg[:, sl, :], in_offset=None,
                            bounds_check=S["bndx"][e_x], oob_is_err=False)
                    P.op("gpsimd", _scat,
                        waits=[("cRH", None), ("cRL", None)] +
                              ([("sZB", ZB[1])] if (qb == 0 and tt == 0 and e_x == 0) else []),
                        inc="sSP", n=16)
                rsg_free[tt] = P.cur("sSP")

        P.op("tensor", lambda e, S: e.matmul(
            ps[0][0:1, 0:1], onc[0:1, :], onc[0:1, :], start=True, stop=True),
            waits=[("cSC", None), ("cG", None), ("cNX", None), ("cRH", None),
                   ("cRL", None), ("cXE", None)], inc="phA")
        P.op("vector", lambda e, S: e.memset(tmp3[0:1, 0, 0:1], 0.0),
             waits=[("cGT", None), ("cNM", None), ("cWO", None), ("cXN", None),
                    ("cQE", None), ("cKE", None), ("cVE", None)], inc="phA")
        P.op("scalar", lambda e, S: e.copy(tmp3[0:1, 1, 0:1], tmp3[0:1, 0, 0:1]),
             waits=[("cGT", None), ("cSE", None), ("cAE", None)], inc="phA")
        att.close()
        PH = [("phA", P.cnt["phA"]), ("sSP", P.cnt["sSP"]), ("sXO", P.cnt["sXO"]),
              ("sGB", P.cnt["sGB"]), ("sGR", P.cnt["sGR"])]

        BF = mybir.dt.bfloat16
        moe = ExitStack()
        msb = lambda name, shape, dt=F32: moe.enter_context(nc.sbuf_tensor(name, shape, dt))
        y_acc = msb("y_acc", [128, 8, THT])
        w1b = msb("w1b", [128, 2, 2, 8, 512], BF)
        w2b = msb("w2b", [128, 2, 2, 4, 1024], BF)
        wf = msb("wf", [128, 4096])
        xnb = msb("xnb", [128, 2, 8, THT], BF)
        h32 = msb("h32", [128, 2, 256])
        hb = msb("hb", [128, 2, 2, 256], BF)
        xrows = moe.enter_context(nc.sbuf_tensor("xrows", [128, 6, RW], mybir.dt.uint16))
        ystage = msb("ystage", [128, 2, 1024])
        gcol = msb("gcol", [128, 6])
        idtu = moe.enter_context(nc.sbuf_tensor("idtu", [128, 6], mybir.dt.uint32))
        idnb = moe.enter_context(nc.sbuf_tensor("idnb", [128, 128], BF))
        psy = moe.enter_context(nc.psum_tensor("psy", [128, 2048], F32))
        psh = moe.enter_context(nc.psum_tensor("psh", [128, 512], F32))
        pstA = moe.enter_context(nc.psum_tensor("pstA", [128, 512], F32))
        pstB = moe.enter_context(nc.psum_tensor("pstB", [128, 512], F32))
        pst = [pstA, pstB]
        xrows_b = xrows[:].bitcast(BF)
        xrows_f = xrows[:].bitcast(F32)
        wf1v = wf[:].rearrange("p (kc m) -> p kc m", kc=8)
        wf2v = wf[:].rearrange("p (mc m) -> p mc m", mc=4)

        P.op("vector", lambda e, S: e.tensor_copy(idnb[:], idn[:]),
             waits=[CW[0]] + list(PH), inc="cYS")

        SCT = ("sSP", P.cnt["sSP"])
        ch_free = {}
        sad_col = {}
        ext_free = {}
        for e_i in range(E):
            for half in range(2):
                base = half * 6
                xn_rdy_nt = []
                for rc in range(6):
                    gw_ = [SCT] + list(PH) + ([ch_free[rc]] if rc in ch_free else []) + \
                          ext_free.get(rc, [])
                    P.op("sync", lambda e, S, e_i=e_i, rc=rc, base=base: e.dma_start(
                        out=xrows[:, rc, :], in_=xbuf_g[e_i, :, base + rc, :]),
                        waits=gw_, inc="sGA", n=16)
                    ga_rdy = P.cur("sGA")
                    for kc in range(8):
                        for hl in range(2):
                            pb = (rc * 16 + kc * 2 + hl) % 2
                            P.op("tensor", lambda e, S, rc=rc, kc=kc, hl=hl, pb=pb: e.matmul(
                                pst[pb][:, 0:64].bitcast(BF),
                                xrows_b[:, rc, hl * 1024 + kc * 128:hl * 1024 + kc * 128 + 128],
                                idnb[:], is_transpose=True),
                                waits=[ga_rdy, ("cYS", None)] +
                                      [("cXG", max(0, P.cnt["cXG"] - 1))],
                                inc="cXT")
                            P.op("vector", lambda e, S, rc=rc, kc=kc, hl=hl, pb=pb: e.tensor_copy(
                                xnb[:, hl, kc, rc * 128:rc * 128 + 128],
                                pst[pb][:, 0:64].bitcast(BF)),
                                waits=[P.cur("cXT")], inc="cXG")
                    P.op("scalar", lambda e, S, e_i=e_i, rc=rc: e.copy(
                        gcol[:, rc:rc + 1], xrows_f[:, rc, 1024 + e_i:1025 + e_i]),
                        waits=[ga_rdy, ("cYS", None), ("cXF", max(0, P.cnt["cXF"] - 15))], inc="cXF")
                    P.op("vector", lambda e, S, rc=rc: e.tensor_copy(
                        idtu[:, rc:rc + 1], xrows_f[:, rc, 1032:1033]),
                        waits=[ga_rdy] + ([sad_col[rc]] if rc in sad_col else []),
                        inc="cXG")
                    ext_free[rc] = [P.cur("cXF"), P.cur("cXG")]
                    xn_rdy_nt.append([P.cur("cXG"), P.cur("cXF")])
                xn_rdy_nt = [xn_rdy_nt[2 * i + 1][0] for i in range(3)]
                ye_base_box = [P.cnt["cYE"] - 8]
                for mg in range(8):
                    sl = mg % 2
                    sfw = slot_free.get(("w1", sl), [])
                    if half == 0:
                        P.op("sync", lambda e, S, e_i=e_i, mg=mg: e.dma_start(
                            out=wf[:], in_=w1_r[e_i, :, :, mg * 512:mg * 512 + 512]),
                            waits=list(PH) + [("cWS", None)], inc="sW1", n=16)
                        w1f_rdy = P.cur("sW1")
                        P.op("gpsimd", lambda e, S, sl=sl: e.tensor_copy(
                            w1b[:, sl, 0, :, :], wf1v),
                            waits=[w1f_rdy] + sfw, inc="cWS")
                        P.op("gpsimd", lambda e, S, sl=sl: e.tensor_tensor(
                            w1b[:, sl, 1, :, :], wf1v, w1b[:, sl, 0, :, :],
                            op=AL.subtract), waits=[P.cur("cWS")], inc="cWS")
                        w1_rdy = ("cWS", P.cnt["cWS"])
                        P.op("sync", lambda e, S, e_i=e_i, mg=mg, sl=sl: e.dma_start(
                            out=w1hd[e_i, mg], in_=w1b[:, sl, 0, :, :]),
                            waits=[w1_rdy], inc="sWB", n=16)
                        P.op("sync", lambda e, S, e_i=e_i, mg=mg, sl=sl: e.dma_start(
                            out=w1ld[e_i, mg], in_=w1b[:, sl, 1, :, :]),
                            inc="sWB", n=16)
                        P.op("sync", lambda e, S, e_i=e_i, mg=mg: e.dma_start(
                            out=wf[:], in_=w2_r[e_i, :, 4 * mg:4 * mg + 4, :]),
                            waits=[("cWS", None)], inc="sW2", n=16)
                        w2f_rdy = P.cur("sW2")
                        P.op("gpsimd", lambda e, S, sl=sl: e.tensor_copy(
                            w2b[:, sl, 0, :, :], wf2v),
                            waits=[w2f_rdy] + slot_free.get(("w2", sl), []), inc="cWS")
                        P.op("gpsimd", lambda e, S, sl=sl: e.tensor_tensor(
                            w2b[:, sl, 1, :, :], wf2v, w2b[:, sl, 0, :, :],
                            op=AL.subtract), waits=[P.cur("cWS")], inc="cWS")
                        w2_rdy = ("cWS", P.cnt["cWS"])
                        P.op("sync", lambda e, S, e_i=e_i, mg=mg, sl=sl: e.dma_start(
                            out=w2hd[e_i, mg], in_=w2b[:, sl, 0, :, :]),
                            waits=[w2_rdy], inc="sWB", n=16)
                        P.op("sync", lambda e, S, e_i=e_i, mg=mg, sl=sl: e.dma_start(
                            out=w2ld[e_i, mg], in_=w2b[:, sl, 1, :, :]),
                            inc="sWB", n=16)
                        slot_free[("w1", sl)] = [("sWB", P.cnt["sWB"] - 32)]
                        slot_free[("w2", sl)] = [("sWB", P.cnt["sWB"])]
                    else:
                        P.op("sync", lambda e, S, e_i=e_i, mg=mg, sl=sl: e.dma_start(
                            out=w1b[:, sl, 0, :, :], in_=w1hd[e_i, mg]),
                            waits=list(PH) + sfw, inc="sW1", n=16)
                        P.op("sync", lambda e, S, e_i=e_i, mg=mg, sl=sl: e.dma_start(
                            out=w1b[:, sl, 1, :, :], in_=w1ld[e_i, mg]),
                            inc="sW1", n=16)
                        w1_rdy = P.cur("sW1")
                        P.op("sync", lambda e, S, e_i=e_i, mg=mg, sl=sl: e.dma_start(
                            out=w2b[:, sl, 0, :, :], in_=w2hd[e_i, mg]),
                            waits=slot_free.get(("w2", sl), []), inc="sW2", n=16)
                        P.op("sync", lambda e, S, e_i=e_i, mg=mg, sl=sl: e.dma_start(
                            out=w2b[:, sl, 1, :, :], in_=w2ld[e_i, mg]),
                            inc="sW2", n=16)
                        w2_rdy = P.cur("sW2")
                    for nt in range(3):
                        ntc = nt * 256

                        def emit_h(m, nt=nt, ntc=ntc, sl=sl, w1_rdy=w1_rdy):
                            hs = (nt * 4 + m) % 2
                            for kc in range(8):
                                for hp in range(3):
                                    wi, xi = ((0, 0), (1, 0), (0, 1))[hp]
                                    P.op("tensor", lambda e, S, sl=sl, m=m, kc=kc, ntc=ntc, hs=hs, wi=wi, xi=xi, hp=hp: e.matmul(
                                        psh[:, hs * 256:hs * 256 + 256],
                                        w1b[:, sl, wi, kc, m * 128:m * 128 + 128],
                                        xnb[:, xi, kc, ntc:ntc + 256],
                                        start=(kc == 0 and hp == 0),
                                        stop=(kc == 7 and hp == 2),
                                        skip_group_check=True),
                                        waits=([w1_rdy, xn_rdy_nt[nt],
                                                ("cHE", None)]
                                               if (kc == 0 and hp == 0) else []),
                                        inc=("cH" if (kc == 7 and hp == 2) else None))
                            return P.cur("cH")

                        def emit_hpost(m, h_mm, nt=nt, e_i=e_i, mg=mg):
                            hs = (nt * 4 + m) % 2
                            P.op("scalar", lambda e, S, e_i=e_i, mg=mg, m=m, hs=hs: e.activation(
                                h32[:, hs, :], psh[:, hs * 256:hs * 256 + 256],
                                AF.Relu, bias=b1_sb[:, e_i, mg * 4 + m:mg * 4 + m + 1]),
                                waits=[h_mm, ("cHL", max(0, P.cnt["cHL"] - 1))], inc="cSQ")
                            P.op("scalar", lambda e, S, hs=hs: e.activation(
                                hb[:, hs, 0, :], h32[:, hs, :], AF.Copy),
                                waits=[P.cur("cSQ"), ("cY", max(0, P.cnt["cY"] - 1))], inc="cHE")
                            he_rdy = P.cur("cHE")
                            P.op("vector", lambda e, S, hs=hs: e.tensor_tensor(
                                hb[:, hs, 1, :], h32[:, hs, :], hb[:, hs, 0, :],
                                op=AL.subtract),
                                waits=[he_rdy], inc="cHL")
                            return he_rdy, P.cur("cHL")

                        def emit_y(m, he_rdy, hl_rdy, nt=nt, sl=sl, w2_rdy=w2_rdy):
                            hs = (nt * 4 + m) % 2
                            for j in range(8):
                                for yp in range(3):
                                    wi, hi_ = ((0, 0), (1, 0), (0, 1))[yp]
                                    P.op("tensor", lambda e, S, sl=sl, m=m, j=j, hs=hs, wi=wi, hi_=hi_, yp=yp: e.matmul(
                                        psy[:, j * 256:j * 256 + 256],
                                        w2b[:, sl, wi, m, j * 128:j * 128 + 128],
                                        hb[:, hs, hi_, :],
                                        start=(m == 0 and j % 2 == 0 and yp == 0),
                                        stop=(m == 3 and yp == 2),
                                        skip_group_check=True),
                                        waits=([w2_rdy, he_rdy, hl_rdy] if (j == 0 and yp == 0) else []) +
                                              ([("cYE", ye_base_box[0] + (j | 1) + 1)]
                                               if (m == 0 and yp == 0) else []),
                                        inc=("cY" if (j == 7 and yp == 2) else None))

                        h_mm = emit_h(0)
                        hpost = emit_hpost(0, h_mm)
                        for m in range(4):
                            if m < 3:
                                nxt_mm = emit_h(m + 1)
                            emit_y(m, hpost[0], hpost[1])
                            if m < 3:
                                hpost = emit_hpost(m + 1, nxt_mm)
                        y_pair = P.cur("cY")
                        ye_base_box[0] = P.cnt["cYE"]
                        for j in range(8):
                            if mg == 0:
                                P.op("vector", lambda e, S, j=j, ntc=ntc, e_i=e_i: e.tensor_scalar(
                                    out=y_acc[:, j, ntc:ntc + 256],
                                    in0=psy[:, j * 256:j * 256 + 256],
                                    scalar1=b2_sb[:, e_i, j:j + 1], scalar2=None,
                                    op0=AL.add),
                                    waits=[y_pair] + ([("cYS", None)] if j == 0 else []),
                                    inc="cYE")
                            else:
                                P.op("vector", lambda e, S, j=j, ntc=ntc: e.tensor_tensor(
                                    y_acc[:, j, ntc:ntc + 256], psy[:, j * 256:j * 256 + 256],
                                    y_acc[:, j, ntc:ntc + 256], op=AL.add),
                                    waits=[y_pair], inc="cYE")
                    if half == 0:
                        slot_free[("w1", sl)] = slot_free[("w1", sl)] + [("cH", P.cnt["cH"])]
                        slot_free[("w2", sl)] = slot_free[("w2", sl)] + [("cY", P.cnt["cY"])]
                    else:
                        slot_free[("w1", sl)] = [("cH", P.cnt["cH"])]
                        slot_free[("w2", sl)] = [("cY", P.cnt["cY"])]
                yacc_rdy = P.cur("cYE")
                for rc in range(6):
                    sl2 = rc % 2
                    for j in range(8):
                        pb = (rc * 8 + j) % 2
                        P.op("tensor", lambda e, S, rc=rc, j=j, pb=pb: e.matmul(
                            pst[pb][:, 0:128], y_acc[:, j, rc * 128:rc * 128 + 128],
                            idn[:], is_transpose=True),
                            waits=([yacc_rdy] if (rc == 0 and j == 0) else []) +
                                  [("cYS", max(0, P.cnt["cYS"] - 1))],
                            inc="cYT")
                        P.op("vector", lambda e, S, rc=rc, j=j, pb=pb, sl2=sl2: e.tensor_scalar(
                            out=ystage[:, sl2, j * 128:j * 128 + 128],
                            in0=pst[pb][:, 0:128], scalar1=gcol[:, rc:rc + 1],
                            scalar2=None, op0=AL.mult),
                            waits=[P.cur("cYT"), ("cXF", None)] +
                              ([("sAD", max(0, P.cnt["sAD"] - 16))] if j == 0 else []),
                            inc="cYS")
                    P.op("gpsimd", lambda e, S, rc=rc, sl2=sl2: e.indirect_dma_start(
                        out=out, out_offset=bass.IndirectOffsetOnAxis(ap=idtu[:, rc:rc + 1], axis=0),
                        in_=ystage[:, sl2, :], in_offset=None, compute_op=AL.add,
                        bounds_check=S["bndo"], oob_is_err=False),
                        waits=[("cYS", None), ("sAD", None), PH[2]],
                        inc="sAD", n=16)
                    sad_col[rc] = P.cur("sAD")
                ch_free[0] = None
                for rc in range(6):
                    ch_free[rc] = ("cXT", P.cnt["cXT"])

        P.op("gpsimd", lambda e, S: None,
             waits=[("sAD", P.cnt["sAD"])] + PH[1:])
        moe.close()

        with nc.Block() as blk:
            for eng_name in ENGINES:
                dec = getattr(blk, eng_name)

                def make(eng_name=eng_name):
                    def body(e):
                        for en, f in P.steps:
                            if en == eng_name:
                                f(e, S)
                    return body
                dec(make())
    return nc


def host_prep(inputs):
    Wq = np.asarray(inputs["L0_attn_Wq"]); bq = np.asarray(inputs["L0_attn_bq"])
    Wk = np.asarray(inputs["L0_attn_Wk"]); bk = np.asarray(inputs["L0_attn_bk"])
    Wv = np.asarray(inputs["L0_attn_Wv"]); bv = np.asarray(inputs["L0_attn_bv"])
    Wo = np.asarray(inputs["L0_attn_Wo"]); bo = np.asarray(inputs["L0_attn_bo"])
    gw = np.asarray(inputs["L0_ffn_gate_w"]); gb = np.asarray(inputs["L0_ffn_gate_b"])
    w1 = np.asarray(inputs["L0_ffn_w1"]); b1 = np.asarray(inputs["L0_ffn_b1"])
    w2 = np.asarray(inputs["L0_ffn_w2"]); b2 = np.asarray(inputs["L0_ffn_b2"])
    x = np.asarray(inputs["L0_attn_x"])

    p = np.arange(128)
    permq = np.zeros(1024, np.int64)
    for c in range(4):
        h = 4 * c + p // 32
        i = p % 32
        permq[128 * c + p] = 64 * h + 2 * i
        permq[128 * (4 + c) + p] = 64 * h + 2 * i + 1
    g_ = p // 32
    i_ = p % 32
    permk = np.concatenate([64 * g_ + 2 * i_, 64 * g_ + 2 * i_ + 1])
    permo = np.zeros(1024, np.int64)
    for c in range(4):
        h = 4 * c + p // 32
        d_ = p % 32
        permo[128 * c + p] = 64 * h + d_
        permo[128 * (4 + c) + p] = 64 * h + 32 + d_

    Wqp = np.ascontiguousarray(Wq[:, permq]); bqp = bq[permq]
    Wkp = np.ascontiguousarray(Wk[:, permk]); bkp = bk[permk]
    Wop = np.ascontiguousarray(Wo[permo, :])

    bq_sb = np.ascontiguousarray(bqp.reshape(8, 128).T).astype(np.float32)
    bkv = np.stack([bkp[:128], bkp[128:], bv[:128], bv[128:]], axis=1).astype(np.float32)
    bo_sb = np.ascontiguousarray(bo.reshape(8, 128).T).astype(np.float32)
    gw_sb = np.ascontiguousarray(gw.T.reshape(8, 128, 8).transpose(1, 0, 2)).astype(np.float32)
    gb_sb = np.ascontiguousarray(gb.reshape(8, 1)).astype(np.float32)
    b1_sb = np.ascontiguousarray(b1.reshape(8, 32, 128).transpose(2, 0, 1)).astype(np.float32)
    b2_sb = np.ascontiguousarray(b2.reshape(8, 8, 128).transpose(2, 0, 1)).astype(np.float32)

    pos = np.arange(T, dtype=np.float64)
    freq = np.exp(-(2.0 * np.arange(32, dtype=np.float64) / 64.0) * np.log(10000.0))
    fr = np.tile(freq, 4)
    ang = fr[:, None] * pos[None, :]
    cosd = np.cos(ang).astype(np.float32)
    sind = np.sin(ang).astype(np.float32)
    import ml_dtypes
    mskd = np.zeros((128, 4, 512), np.float32)
    ar = np.arange(512)
    for j in range(4):
        for pp in range(128):
            mskd[pp, j, :] = np.where(pp + 128 * j <= ar, 0.0, -1e9)
    mskd = mskd.astype(ml_dtypes.bfloat16)
    idnd = np.eye(128, dtype=np.float32)
    selmd = np.zeros((128, 128), np.float32)
    for pp in range(128):
        selmd[32 * (pp // 32), pp] = 1.0

    tkdd = (np.arange(32)[None, :] * 128 + np.arange(128)[:, None]).astype(np.float32)
    shared = dict(wq=Wqp, wk=Wkp, wv=np.ascontiguousarray(Wv), wo=Wop, tkdd=tkdd,
                  bqd=bq_sb, bkvd=np.ascontiguousarray(bkv), bod=bo_sb,
                  gwd=gw_sb, gbd=gb_sb, b1d=b1_sb, b2d=b2_sb, w1d=w1, w2d=w2,
                  cosd=cosd, sind=sind, mskd=mskd, idnd=idnd, selmd=selmd)
    xs = [np.ascontiguousarray(x[2 * c:2 * c + 2].reshape(NSEQ * T, D).T)
          for c in range(NCORES)]
    return shared, xs


def kernel(**inputs):
    shared, xs = host_prep(inputs)
    nc = build_nc()
    in_maps = [dict(shared, xin=xs[c]) for c in range(NCORES)]
    res = run_bass_kernel_spmd(nc, in_maps, core_ids=list(range(NCORES)))
    outs = [np.asarray(res.results[c]["out"]).reshape(NSEQ, T, D)
            for c in range(NCORES)]
    return np.concatenate(outs, axis=0)


# version 11
# speedup vs baseline: 1.0657x; 1.0248x over previous
import numpy as np
from contextlib import ExitStack
import concourse.bass as bass
import concourse.mybir as mybir
from concourse.bass_utils import run_bass_kernel_spmd

F32 = mybir.dt.float32
AF = mybir.ActivationFunctionType
AL = mybir.AluOpType

NCORES = 8
NSEQ, T, D = 2, 2048, 1024
NQB, QBT = 4, 1024
E, H = 8, 4096
NTH, THT = 4, 768

ENGINES = ("sync", "tensor", "vector", "scalar", "gpsimd")


class Prog:
    def __init__(self):
        self.steps = []
        self.cnt = {}
        self._waited = {e: {} for e in ENGINES}

    def sem(self, name):
        self.cnt.setdefault(name, 0)

    def op(self, eng, build, waits=(), inc=None, n=1):
        wl = []
        for s, v in waits:
            val = self.cnt[s] if v is None else v
            if val <= 0:
                continue
            if self._waited[eng].get(s, -1) >= val:
                continue
            self._waited[eng][s] = val
            wl.append((s, val))
        if inc is not None:
            self.cnt[inc] += n

        def f(e, S, wl=wl, build=build, inc=inc, n=n):
            for s, v in wl:
                e.wait_ge(S[s], v)
            ins = build(e, S)
            if inc is not None:
                assert ins is not None, "inc on op that returned no instruction"
                ins.then_inc(S[inc], n)
        self.steps.append((eng, f))

    def cur(self, sem):
        return (sem, self.cnt[sem])


def build_nc():
    nc = bass.Bass("TRN2", target_bir_lowering=False)
    P = Prog()

    def din(name, shape):
        return nc.dram_tensor(name, shape, F32, kind="ExternalInput").ap()
    xin = din("xin", [D, NQB * QBT])
    wq, wk, wv, wo = din("wq", [D, D]), din("wk", [D, 256]), din("wv", [D, 256]), din("wo", [D, D])
    bqd, bkvd, bod = din("bqd", [128, 8]), din("bkvd", [128, 4]), din("bod", [128, 8])
    gwd, gbd = din("gwd", [128, 8, 8]), din("gbd", [8, 1])
    b1d, b2d = din("b1d", [128, 8, 32]), din("b2d", [128, 8, 8])
    w1d, w2d = din("w1d", [E, D, H]), din("w2d", [E, H, D])
    cosd, sind = din("cosd", [128, T]), din("sind", [128, T])
    mskd = nc.dram_tensor("mskd", [128, 4, 512], mybir.dt.bfloat16, kind="ExternalInput").ap()
    idnd, selmd = din("idnd", [128, 128]), din("selmd", [128, 128])
    tkdd = din("tkdd", [128, 32])
    BF16 = mybir.dt.bfloat16
    U16 = mybir.dt.uint16
    U32 = mybir.dt.uint32
    RW = 2080
    CAP = 1536
    out = nc.dram_tensor("out", [NQB * QBT, D], F32, kind="ExternalOutput").ap()
    w1hd = nc.dram_tensor("w1hd", [E, 8, 128, 8, 512], BF16, kind="Internal").ap()
    w1ld = nc.dram_tensor("w1ld", [E, 8, 128, 8, 512], BF16, kind="Internal").ap()
    w2hd = nc.dram_tensor("w2hd", [E, 8, 128, 4, 1024], BF16, kind="Internal").ap()
    w2ld = nc.dram_tensor("w2ld", [E, 8, 128, 4, 1024], BF16, kind="Internal").ap()
    xbuf = nc.dram_tensor("xbuf", [E, CAP, RW], U16, kind="Internal").ap()

    xin_r = xin.rearrange("(kc p) n -> p kc n", p=128)
    xbuf_z = xbuf.rearrange("e c w -> (e c w)")
    xbuf_g = xbuf.rearrange("e (rc p) w -> e p rc w", p=128)
    wq_r = wq.rearrange("(kc p) m -> p kc m", p=128)
    wk_r = wk.rearrange("(kc p) m -> p kc m", p=128)
    wv_r = wv.rearrange("(kc p) m -> p kc m", p=128)
    wo_r = wo.rearrange("(kc p) m -> p kc m", p=128)
    w1_r = w1d.rearrange("e (kc p) m -> e p kc m", p=128)
    w2_r = w2d.rearrange("e (mc p) m -> e p mc m", p=128)

    with ExitStack() as top:
        sb = lambda name, shape: top.enter_context(nc.sbuf_tensor(name, shape, F32))
        mskb = top.enter_context(nc.sbuf_tensor("mskb", [128, 4, 512], mybir.dt.bfloat16))
        idn = sb("idn", [128, 128])
        selm = sb("selm", [128, 128])
        onc = sb("onc", [128, 1])
        onr = sb("onr", [1, 128])
        epsb = sb("epsb", [128, 1])
        bq_sb = sb("bq_sb", [128, 8])
        bkv_sb = sb("bkv_sb", [128, 4])
        bo_sb = sb("bo_sb", [128, 8])
        gw_sb = sb("gw_sb", [128, 8, 8])
        gb_sb = sb("gb_sb", [8, 1])
        b1_sb = sb("b1_sb", [128, 8, 32])
        b2_sb = sb("b2_sb", [128, 8, 8])

        S = {}
        semnames = ["sC", "sX", "sWQ", "sWKV", "sWO", "sCS", "sW1", "sW2", "sXN",
                    "sSP", "sXO", "sGB", "sGR",
                    "cSQ", "cST", "cBC", "cXN", "cQP", "cQE", "cKV", "cKE",
                    "cVT", "cVE", "cKR", "cSC", "cMK", "cEX", "cAV", "cAE",
                    "cSE", "cRC", "cNX", "cNM", "cWO", "cG", "cGT", "phA",
                    "cH", "cHE", "cY", "cYE", "cSTG", "cVv", "cRV", "cSV", "cIN", "cWS", "cHL", "sWB",
                    "sZB", "cXT", "cXE", "cRT", "cRH", "cRL", "sAD", "sGA", "cYT", "cYS", "cXF", "cXG"] + [f"sA{r}" for r in range(16)]
        for name in semnames:
            S[name] = top.enter_context(nc.semaphore(name))
            P.sem(name)

        def _setup_regs(e, S):
            S["bndx"] = []
            for i in range(E):
                r = e.alloc_register(f"bnd{i}")
                e.reg_mov(r, (i + 1) * CAP - 1)
                S["bndx"].append(r)
            r = e.alloc_register("bndo")
            e.reg_mov(r, NQB * QBT - 1)
            S["bndo"] = r
        P.op("gpsimd", lambda e, S: _setup_regs(e, S) or None)

        def dma(qeng, dst, src, inc, waits=()):
            P.op(qeng, lambda e, S, d=dst, s=src: e.dma_start(out=d, in_=s),
                 waits=waits, inc=inc, n=16)

        for dst, src in ((mskb[:], mskd), (idn[:], idnd), (selm[:], selmd),
                         (bq_sb[:], bqd), (bkv_sb[:], bkvd), (bo_sb[:], bod),
                         (gw_sb[:], gwd), (gb_sb[:], gbd), (b1_sb[:], b1d),
                         (b2_sb[:], b2d)):
            dma("sync", dst, src, "sC")
        P.op("vector", lambda e, S: e.memset(onc[:], 1.0), inc="cIN")
        P.op("vector", lambda e, S: e.memset(epsb[:], 1e-5), inc="cIN")
        P.op("vector", lambda e, S: e.memset(onr[:], 1.0), inc="cIN")
        CW = [P.cur("sC"), P.cur("cIN")]

        att = ExitStack()
        asb = lambda name, shape: att.enter_context(nc.sbuf_tensor(name, shape, F32))
        xq = asb("xq", [128, 8, QBT])
        qa = asb("qa", [128, 8, QBT])
        ksb = asb("ksb", [128, 2, T])
        krep = asb("krep", [128, 2, T])
        vsb = asb("vsb", [128, 2, 2, 512])
        vtm = asb("vtm", [128, 4, 16, 66])
        expb = asb("expb", [128, 2, 4, 512])
        csb = asb("csb", [128, 2, 512])
        wqs = asb("wqs", [128, 2, 8, 128])
        wkvs = asb("wkvs", [128, 2, 4, 128])
        tmp3 = asb("tmp3", [128, 3, 512])
        xnt = asb("xnt", [128, 2, 512])
        meanb = asb("meanb", [128, 512])
        rstdb = asb("rstdb", [128, 512])
        sume = asb("sume", [128, QBT])
        reci = asb("reci", [128, 1, QBT])
        recb = asb("recb", [128, 512])
        lgq = asb("lgq", [8, QBT])
        gsct = asb("gsct", [128, 64])
        gsc2 = asb("gsc2", [128, 2, 64])
        gsm = asb("gsm", [128, 4, 8])
        xts = asb("xts", [128, 2, 1024])
        rsg = att.enter_context(nc.sbuf_tensor("rsg", [128, 2, 2080], mybir.dt.uint16))
        sidxT = asb("sidxT", [128, 8, 8])
        idxu8 = att.enter_context(nc.sbuf_tensor("idxu8", [128, 2, 8], mybir.dt.uint32))
        carry = asb("carry", [8, 1])
        ebias = asb("ebias", [128, 8])
        tkd = asb("tkd", [128, 32])
        ps = [att.enter_context(nc.psum_tensor(f"ps{i}", [128, 512], F32)) for i in range(8)]
        statm = reci[0:1, 0, 0:512]
        statm2 = reci[0:1, 0, 512:1024]
        statv = reci[0:1, 0, 512:1024]

        P.op("vector", lambda e, S: e.memset(sume[:], 1.0), inc="cIN")
        P.op("vector", lambda e, S: e.memset(carry[:], 0.0), inc="cIN")
        P.op("gpsimd", lambda e, S: e.iota(ebias[:], [[CAP, 8]], base=0,
             channel_multiplier=0, allow_small_or_imprecise_dtypes=True), inc="cWS")
        P.op("vector", lambda e, S: e.memset(xts[:], 0.0), inc="cIN")
        P.op("vector", lambda e, S: e.memset(rsg[:], 0), inc="cIN")
        dma("sync", tkd[:], tkdd, "sC")
        CW[1] = P.cur("cIN")
        ztot = E * CAP * RW
        zchunk = 128 * 4096
        xts_z16 = xts[:].bitcast(mybir.dt.uint16)
        zoff = 0
        while zoff < ztot:
            n_el = min(zchunk, ztot - zoff)
            npart = n_el // 4096
            P.op("scalar", lambda e, S, zoff=zoff, npart=npart: e.dma_start(
                out=xbuf_z[zoff:zoff + npart * 4096].rearrange("(p n) -> p n", n=4096),
                in_=xts_z16[0:npart, :, :]),
                waits=[CW[1]], inc="sZB", n=16)
            zoff += npart * 4096
        ZB = P.cur("sZB")

        slot_free = {}

        def LN_stats(xsrc_chunk, first_waits, snt):
            st_base = P.cnt["cST"]
            sq_base = P.cnt["cSQ"]
            for kc in range(8):
                sl = kc % 2
                P.op("scalar", lambda e, S, kc=kc, sl=sl, snt=snt: e.activation(
                    vsb[:, sl, snt, :], xsrc_chunk(kc), AF.Square),
                    waits=(first_waits if kc == 0 else []) +
                          [("cQE", None), ("cKE", None)] +
                          ([("cST", st_base)] if kc == 0 else
                           [("cST", st_base + kc - 1)] if kc >= 2 else []),
                    inc="cSQ")
                P.op("tensor", lambda e, S, kc=kc: e.matmul(
                    ps[6][0:1, :], onc[:], xsrc_chunk(kc),
                    start=(kc == 0), stop=(kc == 7)),
                    waits=(first_waits if kc == 0 else []) +
                          ([("cSQ", None), ("cSV", None), ("cNM", None), CW[1]]
                           if kc == 0 else []))
                P.op("tensor", lambda e, S, kc=kc, sl=sl, snt=snt: e.matmul(
                    ps[7][0:1, :], onc[:], vsb[:, sl, snt, :],
                    start=(kc == 0), stop=(kc == 7)),
                    waits=[("cSQ", sq_base + kc + 1)], inc="cST")
            smm = ("cST", st_base + 8)
            P.op("vector", lambda e, S: e.tensor_scalar(
                out=statm, in0=ps[6][0:1, :], scalar1=1.0 / D, scalar2=None,
                op0=AL.mult),
                waits=[smm, ("cQP", None), ("cKV", None), ("cBC", None),
                       ("cXN", None)], inc="cSV")
            P.op("scalar", lambda e, S: e.activation(statm2, statm, AF.Square),
                 waits=[("cSV", None)], inc="cSQ")
            P.op("vector", lambda e, S: e.scalar_tensor_tensor(
                out=statv, in0=ps[7][0:1, :], scalar=1.0 / D, in1=statm2,
                op0=AL.mult, op1=AL.subtract), waits=[("cSQ", None)], inc="cSV")
            P.op("tensor", lambda e, S: e.matmul(
                ps[6][:, :], onr[:], statm, start=True, stop=True),
                waits=[("cSV", None), ("cSQ", None)])
            P.op("tensor", lambda e, S: e.matmul(
                ps[7][:, :], onr[:], statv, start=True, stop=True), inc="cBC")
            bmm = P.cur("cBC")
            P.op("scalar", lambda e, S: e.copy(meanb[:], ps[6][:, :]),
                 waits=[bmm, ("cXN", None)], inc="cSQ")
            P.op("scalar", lambda e, S: e.activation(
                rstdb[:], ps[7][:, :], AF.Sqrt, bias=epsb[:, 0:1]),
                waits=[bmm, ("cXN", None)], inc="cSQ")
            P.op("vector", lambda e, S: e.reciprocal(rstdb[:], rstdb[:]),
                 waits=[("cSQ", None)], inc="cNM")
            return [P.cur("cSQ"), P.cur("cNM")]

        def xn_make(xsrc, dst, ln_rdy, extra, snt):
            P.op("vector", lambda e, S: e.tensor_tensor(
                tmp3[:, 2, :], xsrc, meanb[:], op=AL.subtract),
                waits=ln_rdy + extra, inc="cXN")
            P.op("vector", lambda e, S: e.tensor_tensor(
                dst, tmp3[:, 2, :], rstdb[:], op=AL.mult), inc="cXN")
            return P.cur("cXN")

        def rope(eS, oS, waits0):
            P.op("vector", lambda e, S: e.tensor_tensor(
                tmp3[:, 0, :], eS, csb[:, 1, :], op=AL.mult), waits=waits0)
            P.op("vector", lambda e, S: e.tensor_tensor(
                tmp3[:, 1, :], eS, csb[:, 0, :], op=AL.mult))
            P.op("vector", lambda e, S: e.tensor_tensor(
                tmp3[:, 2, :], oS, csb[:, 1, :], op=AL.mult))
            P.op("vector", lambda e, S: e.tensor_tensor(
                eS, tmp3[:, 1, :], tmp3[:, 2, :], op=AL.subtract))
            P.op("vector", lambda e, S: e.tensor_tensor(
                tmp3[:, 1, :], oS, csb[:, 0, :], op=AL.mult))

            def fin(e, S):
                return e.tensor_tensor(oS, tmp3[:, 0, :], tmp3[:, 1, :], op=AL.add)
            return fin

        sxo_tt = {}
        for qb in range(NQB):
            seq, qbl = qb // 2, qb % 2
            qc0 = qb * QBT
            sc0 = qbl * QBT

            P.op("gpsimd", lambda e, S, c=qc0: e.dma_start(
                out=xq[:], in_=xin_r[:, :, c:c + QBT]),
                waits=[("cXT", None), ("cST", None), ("cXN", None)],
                inc="sX", n=16)
            xrdy = P.cur("sX")

            ln_rdy_nt = {}
            for nt in range(2):
                ntc = nt * 512
                dma("sync", csb[:, 0, :], cosd[:, sc0 + ntc:sc0 + ntc + 512], "sCS",
                    waits=[("cQE", None), ("cKE", None)])
                dma("sync", csb[:, 1, :], sind[:, sc0 + ntc:sc0 + ntc + 512], "sCS")
                cs_rdy = P.cur("sCS")

                if nt == 0:
                    ln_rdy_nt[0] = LN_stats(
                        lambda kc, ntc=ntc: xq[:, kc, ntc:ntc + 512], [xrdy], 0)
                ln_rdy = ln_rdy_nt[nt]

                for kc in range(8):
                    sl = kc % 2
                    wwl = [slot_free[("wq", sl)]] if ("wq", sl) in slot_free else []
                    dma("sync", wqs[:, sl, :, :], wq_r[:, kc, :], "sWQ", waits=wwl)
                    wrdy = P.cur("sWQ")
                    xn_rdy = xn_make(xq[:, kc, ntc:ntc + 512], xnt[:, sl, :], ln_rdy,
                                     [("cQP", P.cnt["cQP"] - 1 if kc >= 2 else 0)], nt)
                    for half in range(2):
                        for c in range(4):
                            last = (half == 1 and c == 3)
                            P.op("tensor", lambda e, S, kc=kc, sl=sl, half=half, c=c: e.matmul(
                                ps[2 * c + half][:, :], wqs[:, sl, 4 * half + c, :],
                                xnt[:, sl, :], start=(kc == 0), stop=(kc == 7)),
                                waits=[wrdy, xn_rdy] +
                                      ([("cQE", None), ("cKE", None), ("cVE", None),
                                        ("cVT", None), ("cVv", None)] if kc == 0 else []),
                                inc=("cQP" if last else None))
                    slot_free[("wq", sl)] = P.cur("cQP")
                qp_mm = P.cur("cQP")
                for c in range(4):
                    P.op("vector", lambda e, S, c=c, ntc=ntc: e.tensor_scalar(
                        out=qa[:, c, ntc:ntc + 512], in0=ps[2 * c][:, :],
                        scalar1=bq_sb[:, c:c + 1], scalar2=None, op0=AL.add),
                        waits=[qp_mm, cs_rdy, ("sSP", None)], inc="cQE")
                    P.op("vector", lambda e, S, c=c, ntc=ntc: e.tensor_scalar(
                        out=qa[:, 4 + c, ntc:ntc + 512], in0=ps[2 * c + 1][:, :],
                        scalar1=bq_sb[:, 4 + c:5 + c], scalar2=None, op0=AL.add),
                        inc="cQE")
                    fin = rope(qa[:, c, ntc:ntc + 512], qa[:, 4 + c, ntc:ntc + 512], [])
                    P.op("vector", fin, inc="cQE")
                q_rope = P.cur("cQE")

                for kc in range(8):
                    sl = kc % 2
                    wwl = [slot_free[("wkv", sl)]] if ("wkv", sl) in slot_free else []
                    dma("sync", wkvs[:, sl, 0:2, :], wk_r[:, kc, :], "sWKV", waits=wwl)
                    dma("sync", wkvs[:, sl, 2:4, :], wv_r[:, kc, :], "sWKV")
                    wrdy = P.cur("sWKV")
                    xn_rdy = xn_make(xq[:, kc, ntc:ntc + 512], xnt[:, sl, :], ln_rdy,
                                     [q_rope, ("cKV", P.cnt["cKV"] - 1 if kc >= 2 else 0)], nt)
                    for m in range(4):
                        P.op("tensor", lambda e, S, kc=kc, sl=sl, m=m: e.matmul(
                            ps[m][:, :], wkvs[:, sl, m, :], xnt[:, sl, :],
                            start=(kc == 0), stop=(kc == 7)),
                            waits=[wrdy, xn_rdy, qp_mm] +
                                  ([("cKE", None), ("cVE", None)] if kc == 0 else []),
                            inc=("cKV" if m == 3 else None))
                    slot_free[("wkv", sl)] = P.cur("cKV")
                kv_mm = P.cur("cKV")
                kcol = sc0 + ntc
                for half in range(2):
                    P.op("vector", lambda e, S, half=half, kcol=kcol: e.tensor_scalar(
                        out=ksb[:, half, kcol:kcol + 512], in0=ps[half][:, :],
                        scalar1=bkv_sb[:, half:half + 1], scalar2=None, op0=AL.add),
                        waits=[kv_mm, cs_rdy, ("cKR", None)], inc="cKE")
                fin = rope(ksb[:, 0, kcol:kcol + 512], ksb[:, 1, kcol:kcol + 512],
                           [("cKE", None)])
                P.op("vector", fin, inc="cKE")
                b = nt % 2
                for ch in range(2):
                    P.op("vector", lambda e, S, ch=ch, b=b: e.tensor_scalar(
                        out=vsb[:, ch, b, :], in0=ps[2 + ch][:, :],
                        scalar1=bkv_sb[:, 2 + ch:3 + ch], scalar2=None, op0=AL.add),
                        waits=[kv_mm, ("cVT", None)], inc="cVE")
                v_ev = P.cur("cVE")
                if nt == 0:
                    ln_rdy_nt[1] = LN_stats(
                        lambda kc: xq[:, kc, 512:1024], [xrdy], 1)
                for ch in range(2):
                    for sub in range(4):
                        pb = 4 + ((ch * 4 + sub) % 2)
                        kt = (sc0 + ntc + sub * 128) // 128
                        P.op("tensor", lambda e, S, ch=ch, sub=sub, pb=pb, b=b: e.matmul(
                            ps[pb][:, 0:128], vsb[:, ch, b, sub * 128:sub * 128 + 128],
                            idn[:], is_transpose=True),
                            waits=[v_ev, CW[0], ("cVv", max(0, P.cnt["cVv"] - 1))],
                            inc="cVT")
                        vmm = P.cur("cVT")
                        P.op("scalar", lambda e, S, ch=ch, kt=kt, pb=pb: e.copy(
                            vtm[:, 2 * ch, kt, 0:64], ps[pb][:, 0:64]), waits=[vmm])
                        P.op("scalar", lambda e, S, ch=ch, kt=kt, pb=pb: e.copy(
                            vtm[:, 2 * ch + 1, kt, 0:64], ps[pb][:, 64:128]),
                            inc="cVv")
            kt0 = sc0 // 128
            P.op("vector", lambda e, S, kt0=kt0: e.memset(
                vtm[:, :, kt0:kt0 + 8, 64:65], 1.0), waits=[("cVT", None)], inc="cVE")
            vtm_rdy = [P.cur("cVT"), P.cur("cVv"), P.cur("cVE")]

            n_keys = (qbl + 1) * QBT
            for c in range(4):
                for j in range(4):
                    for half in range(2):
                        P.op("scalar", lambda e, S, j=j, half=half, c=c, n_keys=n_keys: e.copy(
                            krep[32 * j:32 * j + 32, half, 0:n_keys],
                            ksb[32 * c:32 * c + 32, half, 0:n_keys]),
                            waits=[("cKE", None), ("cSC", None)],
                            inc=("cKR" if (j == 3 and half == 1) else None))
                krep_rdy = P.cur("cKR")
                for qt in range(2):
                    qtc = qt * 512
                    n_kt = (sc0 + qtc + 512) // 128
                    for kt in range(n_kt):
                        for j in range(4):
                            for half in range(2):
                                last = (j == 3 and half == 1)
                                P.op("tensor", lambda e, S, j=j, half=half, kt=kt, c=c, qtc=qtc: e.matmul(
                                    ps[j][:, :],
                                    krep[32 * j:32 * j + 32, half, kt * 128:kt * 128 + 128],
                                    qa[32 * j:32 * j + 32, (c if half == 0 else 4 + c),
                                       qtc:qtc + 512],
                                    start=(half == 0), stop=(half == 1),
                                    tile_position=(32 * j, 0)),
                                    waits=([krep_rdy, q_rope, ("cEX", None),
                                            ("cNX", None)] if (j == 0 and half == 0) else []),
                                    inc=("cSC" if last else None))
                        sc_mm = P.cur("cSC")
                        joff = kt - (n_kt - 4)
                        if joff >= 0:
                            for j in range(4):
                                P.op("vector", lambda e, S, j=j, joff=joff: e.tensor_tensor(
                                    ps[j][:, :], ps[j][:, :], mskb[:, joff, :], op=AL.add),
                                    waits=([sc_mm, CW[0]] if j == 0 else []),
                                    inc=("cMK" if j == 3 else None))
                            exw = [P.cur("cMK")]
                        else:
                            exw = [sc_mm]
                        for j in range(4):
                            P.op("scalar", lambda e, S, j=j, kt=kt: e.activation(
                                expb[:, kt % 2, j, :], ps[j][:, :], AF.Exp, scale=0.125),
                                waits=(exw + [("cAV", max(0, P.cnt["cAV"] - 1))]) if j == 0 else [],
                                inc=("cEX" if j == 3 else None))
                        ex_rdy = P.cur("cEX")
                        for j in range(4):
                            P.op("tensor", lambda e, S, j=j, kt=kt, c=c, n_kt=n_kt: e.matmul(
                                ps[4 + j][0:65, :], vtm[:, c, kt, 0:65],
                                expb[:, kt % 2, j, :],
                                start=(kt == 0), stop=(kt == n_kt - 1)),
                                waits=([ex_rdy] + (vtm_rdy + [("cAE", None)] if kt == 0 else [])) if j == 0 else [],
                                inc=("cAV" if j == 3 else None))
                    av_done = P.cur("cAV")
                    for j in range(4):
                        P.op("scalar", lambda e, S, j=j, qtc=qtc: e.copy(
                            sume[32 * j:32 * j + 1, qtc:qtc + 512],
                            ps[4 + j][64:65, :]),
                            waits=([av_done, ("cRV", None), ("cNX", None)] if j == 0 else []),
                            inc=("cSE" if j == 3 else None))
                        P.op("scalar", lambda e, S, j=j, c=c, qtc=qtc: e.copy(
                            qa[32 * j:32 * j + 32, c, qtc:qtc + 512], ps[4 + j][0:32, :]))
                        P.op("scalar", lambda e, S, j=j, c=c, qtc=qtc: e.copy(
                            qa[32 * j:32 * j + 32, 4 + c, qtc:qtc + 512],
                            ps[4 + j][32:64, :]),
                            inc=("cAE" if j == 3 else None))
                se_rdy = P.cur("cSE")
                ae_rdy = P.cur("cAE")
                P.op("scalar", lambda e, S: e.activation(
                    reci[:, 0, :], sume[:, :], AF.Ln),
                    waits=[se_rdy, ("cNM", None)], inc="cRC")
                P.op("scalar", lambda e, S: e.activation(
                    reci[:, 0, :], reci[:, 0, :], AF.Exp, scale=-1.0), inc="cRC")
                nsc = xts[:, 0, 0:QBT]
                P.op("vector", lambda e, S, nsc=nsc: e.tensor_tensor(
                    nsc, sume[:, :], reci[:, 0, :], op=AL.mult),
                    waits=[("cRC", None), ("sXO", None), ("sZB", None)], inc="cRV")
                P.op("vector", lambda e, S, nsc=nsc: e.tensor_scalar(
                    out=nsc, in0=nsc, scalar1=-1.0, scalar2=2.0,
                    op0=AL.mult, op1=AL.add), inc="cRV")
                P.op("vector", lambda e, S, nsc=nsc: e.tensor_tensor(
                    sume[:, :], reci[:, 0, :], nsc, op=AL.mult), inc="cRV")
                rc_rdy = P.cur("cRV")
                for qt in range(2):
                    qtc = qt * 512
                    P.op("tensor", lambda e, S, qtc=qtc: e.matmul(
                        ps[0][:, :], selm[:], sume[:, qtc:qtc + 512],
                        start=True, stop=True),
                        waits=[rc_rdy, CW[0], ("cEX", None), ("cNX", None)], inc="cSC")
                    P.op("vector", lambda e, S: e.tensor_copy(recb[:], ps[0][:, :]),
                         waits=[("cSC", None)], inc="cNX")
                    for u in range(2):
                        P.op("vector", lambda e, S, c=c, u=u, qtc=qtc: e.tensor_tensor(
                            qa[:, c + 4 * u, qtc:qtc + 512],
                            qa[:, c + 4 * u, qtc:qtc + 512], recb[:], op=AL.mult),
                            waits=([ae_rdy] if u == 0 else []),
                            inc=("cNM" if u == 1 else None))
            nm_rdy = P.cur("cNM")

            for m in range(8):
                sl = m % 2
                wwl = [slot_free[("wo", sl)]] if ("wo", sl) in slot_free else []
                if ("wq", sl) in slot_free:
                    wwl.append(slot_free[("wq", sl)])
                dma("sync", wqs[:, sl, :, :], wo_r[:, :, m * 128:m * 128 + 128],
                    "sWO", waits=wwl)
                wrdy = P.cur("sWO")
                for nt in range(2):
                    pb = 4 + nt
                    for kc in range(8):
                        P.op("tensor", lambda e, S, m=m, sl=sl, nt=nt, kc=kc, pb=pb: e.matmul(
                            ps[pb][:, :], wqs[:, sl, kc, :],
                            qa[:, kc, nt * 512:nt * 512 + 512],
                            start=(kc == 0), stop=(kc == 7)),
                            waits=([wrdy, nm_rdy, ("cWO", max(0, P.cnt["cWO"] - 1))]
                                   if kc == 0 else []),
                            inc=("cG" if kc == 7 else None))
                    womm = P.cur("cG")
                    P.op("vector", lambda e, S, m=m, nt=nt, pb=pb: e.scalar_tensor_tensor(
                        out=xq[:, m, nt * 512:nt * 512 + 512], in0=ps[pb][:, :],
                        scalar=bo_sb[:, m:m + 1],
                        in1=xq[:, m, nt * 512:nt * 512 + 512],
                        op0=AL.add, op1=AL.add), waits=[womm], inc="cWO")
                slot_free[("wo", sl)] = P.cur("cWO")
                slot_free[("wq", sl)] = P.cur("cG")
            xp_rdy = P.cur("cWO")

            ln2_rdy_nt = {0: LN_stats(lambda kc: xq[:, kc, 0:512], [xp_rdy], 0)}
            for nt in range(2):
                ntc = nt * 512
                ln_rdy = ln2_rdy_nt[nt]
                for kc in range(8):
                    xn_rdy = xn_make(xq[:, kc, ntc:ntc + 512], qa[:, kc, ntc:ntc + 512],
                                     ln_rdy, [("cG", None), ("sSP", None)], nt)
                    P.op("tensor", lambda e, S, kc=kc, ntc=ntc: e.matmul(
                        ps[5][0:8, :], gw_sb[:, kc, :], qa[:, kc, ntc:ntc + 512],
                        start=(kc == 0), stop=(kc == 7)),
                        waits=[xn_rdy] + ([CW[0], xp_rdy, ("cGT", None)] if kc == 0 else []),
                        inc=("cG" if kc == 7 else None))
                if nt == 0:
                    ln2_rdy_nt[1] = LN_stats(
                        lambda kc: xq[:, kc, 512:1024], [xp_rdy], 1)
                P.op("vector", lambda e, S, ntc=ntc: e.tensor_scalar(
                    out=lgq[0:8, ntc:ntc + 512], in0=ps[5][0:8, :],
                    scalar1=gb_sb[0:8, 0:1], scalar2=None, op0=AL.add),
                    waits=[("cG", None), ("cGT", None)], inc="cGT")
            lg_rdy = P.cur("cGT")

            def gop(eng, fn, w=()):
                P.op(eng, fn, waits=list(w) + [("cGT", None)], inc="cGT")
            for st_ in range(8):
                gop("tensor", lambda e, S, st_=st_: e.matmul(
                    ps[4][:, 8 * st_:8 * st_ + 8], lgq[0:8, 128 * st_:128 * st_ + 128],
                    idn[0:8, 0:8], is_transpose=True), w=[lg_rdy, ("cAE", None)])
            gop("vector", lambda e, S: e.tensor_copy(gsct[:, :], ps[4][:, 0:64]))
            lgT = gsct.ap() if hasattr(gsct, "ap") else gsct
            gop("vector", lambda e, S: e.tensor_reduce(
                gsm[:, 0, :], gsct[:].rearrange("p (s e) -> p s e", e=8),
                mybir.AxisListType.X, AL.max))
            mxb = gsm[:, 0:1, :].rearrange("p a e -> p e a").broadcast_to((128, 8, 8))
            m2b = gsm[:, 1:2, :].rearrange("p a e -> p e a").broadcast_to((128, 8, 8))
            gop("vector", lambda e, S: e.tensor_tensor(
                gsc2[:, 1, :].rearrange("p (s e) -> p s e", e=8),
                gsct[:].rearrange("p (s e) -> p s e", e=8), mxb, op=AL.is_equal),
                w=[("cRH", None)])
            gop("vector", lambda e, S: e.scalar_tensor_tensor(
                out=gsc2[:, 1, :], in0=gsc2[:, 1, :], scalar=-1e30,
                in1=gsct[:], op0=AL.mult, op1=AL.add))
            gop("vector", lambda e, S: e.tensor_reduce(
                gsm[:, 1, :], gsc2[:, 1, :].rearrange("p (s e) -> p s e", e=8),
                mybir.AxisListType.X, AL.max))
            gop("vector", lambda e, S: e.tensor_tensor(
                gsc2[:, 0, :].rearrange("p (s e) -> p s e", e=8),
                gsct[:].rearrange("p (s e) -> p s e", e=8), mxb, op=AL.subtract))
            gop("scalar", lambda e, S: e.activation(
                gsc2[:, 0, :], gsc2[:, 0, :], AF.Exp))
            gop("vector", lambda e, S: e.tensor_reduce(
                gsm[:, 2, :], gsc2[:, 0, :].rearrange("p (s e) -> p s e", e=8),
                mybir.AxisListType.X, AL.add))
            gop("vector", lambda e, S: e.reciprocal(gsm[:, 3, :], gsm[:, 2, :]))
            rcb = gsm[:, 3:4, :].rearrange("p a e -> p e a").broadcast_to((128, 8, 8))
            gop("vector", lambda e, S: e.tensor_tensor(
                gsc2[:, 0, :].rearrange("p (s e) -> p s e", e=8),
                gsc2[:, 0, :].rearrange("p (s e) -> p s e", e=8), rcb, op=AL.mult))
            gop("vector", lambda e, S: e.tensor_tensor(
                gsc2[:, 1, :].rearrange("p (s e) -> p s e", e=8),
                gsct[:].rearrange("p (s e) -> p s e", e=8), m2b, op=AL.is_ge))
            gop("vector", lambda e, S: e.tensor_tensor(
                gsc2[:, 1, :], gsc2[:, 0, :], gsc2[:, 1, :], op=AL.mult))
            for st_ in range(8):
                pb = 4 + st_ // 4
                gop("tensor", lambda e, S, st_=st_, pb=pb: e.matmul(
                    ps[pb][0:8, 128 * (st_ % 4):128 * (st_ % 4) + 128],
                    gsc2[:, 1, 8 * st_:8 * st_ + 8], idn[:], is_transpose=True))
            for st_ in range(8):
                pb = 4 + st_ // 4
                gop("vector", lambda e, S, st_=st_, pb=pb: e.tensor_copy(
                    lgq[0:8, 128 * st_:128 * st_ + 128],
                    ps[pb][0:8, 128 * (st_ % 4):128 * (st_ % 4) + 128]))
            rnk = reci[0:8, 0, 0:QBT]
            rsel = xts[0:8, 1, 0:QBT]
            ones8 = xnt[0:8, :, :].rearrange("p a b -> p (a b)")
            gop("vector", lambda e, S: e.memset(ones8, 1.0), w=[("cKV", None)])
            gop("vector", lambda e, S: e.tensor_scalar(
                out=rsel, in0=lgq[0:8, :], scalar1=0.0, scalar2=None, op0=AL.is_gt),
                w=[("cRC", None), ("sXO", None)])
            gop("vector", lambda e, S: e.tensor_tensor_scan(
                rnk, ones8, rsel, carry[0:8, 0:1], op0=AL.mult, op1=AL.add))
            gop("vector", lambda e, S: e.tensor_copy(carry[0:8, 0:1], rnk[:, QBT - 1:QBT]))
            gop("vector", lambda e, S: e.scalar_tensor_tensor(
                out=rsel, in0=rsel, scalar=-4194304.0, in1=rnk, op0=AL.mult, op1=AL.add))
            gop("vector", lambda e, S: e.tensor_scalar(
                out=rsel, in0=rsel, scalar1=4194303.0, scalar2=None, op0=AL.add))
            for st_ in range(8):
                gop("tensor", lambda e, S, st_=st_: e.matmul(
                    ps[6][:, 8 * st_:8 * st_ + 8], rsel[:, 128 * st_:128 * st_ + 128],
                    idn[0:8, 0:8], is_transpose=True))
            gop("vector", lambda e, S: e.tensor_copy(sidxT[:].rearrange("p a b -> p (a b)"),
                                                     ps[6][:, 0:64]))

            for tt in range(8):
                tg = qb * 8 + tt
                sl = tt % 2
                for kc in range(8):
                    pb = 2 + ((tt * 8 + kc) % 2)
                    P.op("tensor", lambda e, S, tt=tt, kc=kc, pb=pb: e.matmul(
                        ps[pb][:, 0:128], xq[:, kc, tt * 128:tt * 128 + 128], idn[:],
                        is_transpose=True),
                        waits=([xp_rdy, ("cEX", None)] if (tt == 0 and kc == 0) else []) +
                              [("cXE", max(0, P.cnt["cXE"] - 1))],
                        inc="cXT")
                    P.op("scalar", lambda e, S, tt=tt, kc=kc, pb=pb, sl=sl: e.copy(
                        xts[:, sl, kc * 128:kc * 128 + 128], ps[pb][:, 0:128]),
                        waits=[P.cur("cXT")] +
                              ([("cGT", None), ("cRV", None)] if (kc == 0 and tt == 0) else []) +
                              ([sxo_tt[tg - 2]] if (kc == 0 and tg - 2 in sxo_tt) else []),
                        inc="cXE")
                P.op("gpsimd", lambda e, S, tt=tt, sl=sl, c=qc0: e.dma_start(
                    out=out[c + tt * 128:c + tt * 128 + 128, :], in_=xts[:, sl, :]),
                    waits=[("cXE", None)], inc="sXO", n=16)
                sxo_tt[tg] = P.cur("sXO")

            rsg_b = rsg[:].bitcast(mybir.dt.bfloat16)
            rsg_f = rsg[:].bitcast(F32)
            rsg_free = {}
            for tt in range(8):
                sl = tt % 2
                for kc in range(8):
                    pb = (tt * 8 + kc) % 2
                    P.op("tensor", lambda e, S, tt=tt, kc=kc, pb=pb: e.matmul(
                        ps[pb][:, 0:128], qa[:, kc, tt * 128:tt * 128 + 128], idn[:],
                        is_transpose=True),
                        waits=([("cG", None), ("cNX", None), ("cXN", None),
                                ("cGT", None)] if (tt == 0 and kc == 0) else []) +
                              [("cRL", max(0, P.cnt["cRL"] - 1))],
                        inc="cRT")
                    P.op("scalar", lambda e, S, kc=kc, pb=pb, sl=sl: e.activation(
                        rsg_b[:, sl, kc * 128:kc * 128 + 128], ps[pb][:, 0:128], AF.Copy),
                        waits=[P.cur("cRT")] +
                              ([rsg_free[tt - 2]] if (kc == 0 and tt - 2 in rsg_free) else []),
                        inc="cRH")
                    P.op("vector", lambda e, S, kc=kc, pb=pb, sl=sl: e.tensor_tensor(
                        rsg_b[:, sl, 1024 + kc * 128:1024 + kc * 128 + 128], ps[pb][:, 0:128],
                        rsg_b[:, sl, kc * 128:kc * 128 + 128], op=AL.subtract),
                        waits=[P.cur("cRH")], inc="cRL")
                P.op("scalar", lambda e, S, tt=tt, sl=sl: e.copy(
                    rsg_f[:, sl, 1024:1032], gsc2[:, 1, 8 * tt:8 * tt + 8]),
                    waits=[("cGT", None)], inc="cRH")
                P.op("scalar", lambda e, S, tt=tt, sl=sl, qb=qb: e.copy(
                    rsg_f[:, sl, 1032:1033], tkd[:, qb * 8 + tt:qb * 8 + tt + 1]),
                    waits=[("sC", None)], inc="cRH")
                P.op("vector", lambda e, S, tt=tt, sl=sl: e.tensor_tensor(
                    idxu8[:, sl, :], sidxT[:, tt, :], ebias[:], op=AL.add),
                    waits=[("cGT", None), ("cWS", 1)] +
                          ([rsg_free[tt - 2]] if tt - 2 in rsg_free else []),
                    inc="cRL")
                xbuf_rows = xbuf.rearrange("e c w -> (e c) w")
                for e_x in range(E):
                    def _scat(e, S, e_x=e_x, sl=sl, xr=xbuf_rows):
                        return e.indirect_dma_start(
                            out=xr,
                            out_offset=bass.IndirectOffsetOnAxis(ap=idxu8[:, sl, e_x:e_x + 1], axis=0),
                            in_=rsg[:, sl, :], in_offset=None,
                            bounds_check=S["bndx"][e_x], oob_is_err=False)
                    P.op("gpsimd", _scat,
                        waits=[("cRH", None), ("cRL", None)] +
                              ([("sZB", ZB[1])] if (qb == 0 and tt == 0 and e_x == 0) else []),
                        inc="sSP", n=16)
                rsg_free[tt] = P.cur("sSP")

        P.op("tensor", lambda e, S: e.matmul(
            ps[0][0:1, 0:1], onc[0:1, :], onc[0:1, :], start=True, stop=True),
            waits=[("cSC", None), ("cG", None), ("cNX", None), ("cRH", None),
                   ("cRL", None), ("cXE", None)], inc="phA")
        P.op("vector", lambda e, S: e.memset(tmp3[0:1, 0, 0:1], 0.0),
             waits=[("cGT", None), ("cNM", None), ("cWO", None), ("cXN", None),
                    ("cQE", None), ("cKE", None), ("cVE", None)], inc="phA")
        P.op("scalar", lambda e, S: e.copy(tmp3[0:1, 1, 0:1], tmp3[0:1, 0, 0:1]),
             waits=[("cGT", None), ("cSE", None), ("cAE", None)], inc="phA")
        att.close()
        PH = [("phA", P.cnt["phA"]), ("sSP", P.cnt["sSP"]), ("sXO", P.cnt["sXO"]),
              ("sGB", P.cnt["sGB"]), ("sGR", P.cnt["sGR"])]

        BF = mybir.dt.bfloat16
        moe = ExitStack()
        msb = lambda name, shape, dt=F32: moe.enter_context(nc.sbuf_tensor(name, shape, dt))
        y_acc = msb("y_acc", [128, 8, THT])
        w1b = msb("w1b", [128, 2, 2, 8, 512], BF)
        w2b = msb("w2b", [128, 2, 2, 4, 1024], BF)
        wf = msb("wf", [128, 4096])
        xnb = msb("xnb", [128, 2, 8, THT], BF)
        h32 = msb("h32", [128, 2, 256])
        hb = msb("hb", [128, 2, 2, 256], BF)
        xrows = moe.enter_context(nc.sbuf_tensor("xrows", [128, 6, RW], mybir.dt.uint16))
        ystage = msb("ystage", [128, 2, 1024])
        gcol = msb("gcol", [128, 6])
        idtu = moe.enter_context(nc.sbuf_tensor("idtu", [128, 6], mybir.dt.uint32))
        idnb = moe.enter_context(nc.sbuf_tensor("idnb", [128, 128], BF))
        psy = moe.enter_context(nc.psum_tensor("psy", [128, 2048], F32))
        psh = moe.enter_context(nc.psum_tensor("psh", [128, 512], F32))
        pstA = moe.enter_context(nc.psum_tensor("pstA", [128, 512], F32))
        pstB = moe.enter_context(nc.psum_tensor("pstB", [128, 512], F32))
        pst = [pstA, pstB]
        xrows_b = xrows[:].bitcast(BF)
        xrows_f = xrows[:].bitcast(F32)
        wf1v = wf[:].rearrange("p (kc m) -> p kc m", kc=8)
        wf2v = wf[:].rearrange("p (mc m) -> p mc m", mc=4)

        P.op("vector", lambda e, S: e.tensor_copy(idnb[:], idn[:]),
             waits=[CW[0]] + list(PH), inc="cYS")

        SCT = ("sSP", P.cnt["sSP"])
        ch_free = {}
        sad_col = {}
        ext_free = {}
        for e_i in range(E):
            for half in range(2):
                base = half * 6
                xn_rdy_nt = []
                for rc in range(6):
                    gw_ = [SCT] + list(PH) + ([ch_free[rc]] if rc in ch_free else []) + \
                          ext_free.get(rc, [])
                    P.op("sync", lambda e, S, e_i=e_i, rc=rc, base=base: e.dma_start(
                        out=xrows[:, rc, :], in_=xbuf_g[e_i, :, base + rc, :]),
                        waits=gw_, inc="sGA", n=16)
                    ga_rdy = P.cur("sGA")
                    for kc in range(8):
                        for hl in range(2):
                            pb = (rc * 16 + kc * 2 + hl) % 2
                            P.op("tensor", lambda e, S, rc=rc, kc=kc, hl=hl, pb=pb: e.matmul(
                                pst[pb][:, 0:64].bitcast(BF),
                                xrows_b[:, rc, hl * 1024 + kc * 128:hl * 1024 + kc * 128 + 128],
                                idnb[:], is_transpose=True),
                                waits=[ga_rdy, ("cYS", None)] +
                                      [("cXG", max(0, P.cnt["cXG"] - 1))],
                                inc="cXT")
                            P.op("vector", lambda e, S, rc=rc, kc=kc, hl=hl, pb=pb: e.tensor_copy(
                                xnb[:, hl, kc, rc * 128:rc * 128 + 128],
                                pst[pb][:, 0:64].bitcast(BF)),
                                waits=[P.cur("cXT")], inc="cXG")
                    P.op("scalar", lambda e, S, e_i=e_i, rc=rc: e.copy(
                        gcol[:, rc:rc + 1], xrows_f[:, rc, 1024 + e_i:1025 + e_i]),
                        waits=[ga_rdy, ("cYS", None), ("cXF", max(0, P.cnt["cXF"] - 15))], inc="cXF")
                    P.op("vector", lambda e, S, rc=rc: e.tensor_copy(
                        idtu[:, rc:rc + 1], xrows_f[:, rc, 1032:1033]),
                        waits=[ga_rdy] + ([sad_col[rc]] if rc in sad_col else []),
                        inc="cXG")
                    ext_free[rc] = [P.cur("cXF"), P.cur("cXG")]
                    xn_rdy_nt.append([P.cur("cXG"), P.cur("cXF")])
                xn_rdy_nt = [xn_rdy_nt[2 * i + 1][0] for i in range(3)]
                ye_base_box = [P.cnt["cYE"] - 8]
                for mg in range(8):
                    sl = mg % 2
                    sfw = slot_free.get(("w1", sl), [])
                    if half == 0:
                        P.op("sync", lambda e, S, e_i=e_i, mg=mg: e.dma_start(
                            out=wf[:], in_=w1_r[e_i, :, :, mg * 512:mg * 512 + 512]),
                            waits=list(PH) + [("cWS", None)], inc="sW1", n=16)
                        w1f_rdy = P.cur("sW1")
                        P.op("gpsimd", lambda e, S, sl=sl: e.tensor_copy(
                            w1b[:, sl, 0, :, :], wf1v),
                            waits=[w1f_rdy] + sfw, inc="cWS")
                        P.op("gpsimd", lambda e, S, sl=sl: e.tensor_tensor(
                            w1b[:, sl, 1, :, :], wf1v, w1b[:, sl, 0, :, :],
                            op=AL.subtract), waits=[P.cur("cWS")], inc="cWS")
                        w1_rdy = ("cWS", P.cnt["cWS"])
                        P.op("sync", lambda e, S, e_i=e_i, mg=mg, sl=sl: e.dma_start(
                            out=w1hd[e_i, mg], in_=w1b[:, sl, 0, :, :]),
                            waits=[w1_rdy], inc="sWB", n=16)
                        P.op("sync", lambda e, S, e_i=e_i, mg=mg, sl=sl: e.dma_start(
                            out=w1ld[e_i, mg], in_=w1b[:, sl, 1, :, :]),
                            inc="sWB", n=16)
                        P.op("sync", lambda e, S, e_i=e_i, mg=mg: e.dma_start(
                            out=wf[:], in_=w2_r[e_i, :, 4 * mg:4 * mg + 4, :]),
                            waits=[("cWS", None)], inc="sW2", n=16)
                        w2f_rdy = P.cur("sW2")
                        P.op("gpsimd", lambda e, S, sl=sl: e.tensor_copy(
                            w2b[:, sl, 0, :, :], wf2v),
                            waits=[w2f_rdy] + slot_free.get(("w2", sl), []), inc="cWS")
                        P.op("gpsimd", lambda e, S, sl=sl: e.tensor_tensor(
                            w2b[:, sl, 1, :, :], wf2v, w2b[:, sl, 0, :, :],
                            op=AL.subtract), waits=[P.cur("cWS")], inc="cWS")
                        w2_rdy = ("cWS", P.cnt["cWS"])
                        P.op("sync", lambda e, S, e_i=e_i, mg=mg, sl=sl: e.dma_start(
                            out=w2hd[e_i, mg], in_=w2b[:, sl, 0, :, :]),
                            waits=[w2_rdy], inc="sWB", n=16)
                        P.op("sync", lambda e, S, e_i=e_i, mg=mg, sl=sl: e.dma_start(
                            out=w2ld[e_i, mg], in_=w2b[:, sl, 1, :, :]),
                            inc="sWB", n=16)
                        slot_free[("w1", sl)] = [("sWB", P.cnt["sWB"] - 32)]
                        slot_free[("w2", sl)] = [("sWB", P.cnt["sWB"])]
                    else:
                        P.op("sync", lambda e, S, e_i=e_i, mg=mg, sl=sl: e.dma_start(
                            out=w1b[:, sl, 0, :, :], in_=w1hd[e_i, mg]),
                            waits=list(PH) + sfw, inc="sW1", n=16)
                        P.op("sync", lambda e, S, e_i=e_i, mg=mg, sl=sl: e.dma_start(
                            out=w1b[:, sl, 1, :, :], in_=w1ld[e_i, mg]),
                            inc="sW1", n=16)
                        w1_rdy = P.cur("sW1")
                        P.op("sync", lambda e, S, e_i=e_i, mg=mg, sl=sl: e.dma_start(
                            out=w2b[:, sl, 0, :, :], in_=w2hd[e_i, mg]),
                            waits=slot_free.get(("w2", sl), []), inc="sW2", n=16)
                        P.op("sync", lambda e, S, e_i=e_i, mg=mg, sl=sl: e.dma_start(
                            out=w2b[:, sl, 1, :, :], in_=w2ld[e_i, mg]),
                            inc="sW2", n=16)
                        w2_rdy = P.cur("sW2")
                    for nt in range(3):
                        ntc = nt * 256

                        def emit_h(m, nt=nt, ntc=ntc, sl=sl, w1_rdy=w1_rdy):
                            hs = (nt * 4 + m) % 2
                            for kc in range(8):
                                for hp in range(3):
                                    wi, xi = ((0, 0), (1, 0), (0, 1))[hp]
                                    P.op("tensor", lambda e, S, sl=sl, m=m, kc=kc, ntc=ntc, hs=hs, wi=wi, xi=xi, hp=hp: e.matmul(
                                        psh[:, hs * 256:hs * 256 + 256],
                                        w1b[:, sl, wi, kc, m * 128:m * 128 + 128],
                                        xnb[:, xi, kc, ntc:ntc + 256],
                                        start=(kc == 0 and hp == 0),
                                        stop=(kc == 7 and hp == 2),
                                        skip_group_check=True),
                                        waits=([w1_rdy, xn_rdy_nt[nt],
                                                ("cHE", None)]
                                               if (kc == 0 and hp == 0) else []),
                                        inc=("cH" if (kc == 7 and hp == 2) else None))
                            return P.cur("cH")

                        def emit_hpost(m, h_mm, nt=nt, e_i=e_i, mg=mg):
                            hs = (nt * 4 + m) % 2
                            P.op("scalar", lambda e, S, e_i=e_i, mg=mg, m=m, hs=hs: e.activation(
                                h32[:, hs, :], psh[:, hs * 256:hs * 256 + 256],
                                AF.Relu, bias=b1_sb[:, e_i, mg * 4 + m:mg * 4 + m + 1]),
                                waits=[h_mm, ("cHL", max(0, P.cnt["cHL"] - 1))], inc="cSQ")
                            P.op("scalar", lambda e, S, hs=hs: e.activation(
                                hb[:, hs, 0, :], h32[:, hs, :], AF.Copy),
                                waits=[P.cur("cSQ"), ("cY", max(0, P.cnt["cY"] - 1))], inc="cHE")
                            he_rdy = P.cur("cHE")
                            P.op("vector", lambda e, S, hs=hs: e.tensor_tensor(
                                hb[:, hs, 1, :], h32[:, hs, :], hb[:, hs, 0, :],
                                op=AL.subtract),
                                waits=[he_rdy], inc="cHL")
                            return he_rdy, P.cur("cHL")

                        def emit_y(m, he_rdy, hl_rdy, nt=nt, sl=sl, w2_rdy=w2_rdy):
                            hs = (nt * 4 + m) % 2
                            for j in range(8):
                                for yp in range(3):
                                    wi, hi_ = ((0, 0), (1, 0), (0, 1))[yp]
                                    P.op("tensor", lambda e, S, sl=sl, m=m, j=j, hs=hs, wi=wi, hi_=hi_, yp=yp: e.matmul(
                                        psy[:, j * 256:j * 256 + 256],
                                        w2b[:, sl, wi, m, j * 128:j * 128 + 128],
                                        hb[:, hs, hi_, :],
                                        start=(m == 0 and j % 2 == 0 and yp == 0),
                                        stop=(m == 3 and yp == 2),
                                        skip_group_check=True),
                                        waits=([w2_rdy, he_rdy, hl_rdy] if (j == 0 and yp == 0) else []) +
                                              ([("cYE", ye_base_box[0] + (j | 1) + 1)]
                                               if (m == 0 and yp == 0) else []),
                                        inc=("cY" if (j == 7 and yp == 2) else None))

                        h_mm = emit_h(0)
                        hpost = emit_hpost(0, h_mm)
                        for m in range(4):
                            if m < 3:
                                nxt_mm = emit_h(m + 1)
                            emit_y(m, hpost[0], hpost[1])
                            if m < 3:
                                hpost = emit_hpost(m + 1, nxt_mm)
                        y_pair = P.cur("cY")
                        ye_base_box[0] = P.cnt["cYE"]
                        for j in range(8):
                            if mg == 0:
                                P.op("vector", lambda e, S, j=j, ntc=ntc, e_i=e_i: e.tensor_scalar(
                                    out=y_acc[:, j, ntc:ntc + 256],
                                    in0=psy[:, j * 256:j * 256 + 256],
                                    scalar1=b2_sb[:, e_i, j:j + 1], scalar2=None,
                                    op0=AL.add),
                                    waits=[y_pair] + ([("cYS", None)] if j == 0 else []),
                                    inc="cYE")
                            else:
                                P.op("vector", lambda e, S, j=j, ntc=ntc: e.tensor_tensor(
                                    y_acc[:, j, ntc:ntc + 256], psy[:, j * 256:j * 256 + 256],
                                    y_acc[:, j, ntc:ntc + 256], op=AL.add),
                                    waits=[y_pair], inc="cYE")
                    if half == 0:
                        slot_free[("w1", sl)] = slot_free[("w1", sl)] + [("cH", P.cnt["cH"])]
                        slot_free[("w2", sl)] = slot_free[("w2", sl)] + [("cY", P.cnt["cY"])]
                    else:
                        slot_free[("w1", sl)] = [("cH", P.cnt["cH"])]
                        slot_free[("w2", sl)] = [("cY", P.cnt["cY"])]
                yacc_rdy = P.cur("cYE")
                for rc in range(6):
                    sl2 = rc % 2
                    for j in range(8):
                        pb = (rc * 8 + j) % 2
                        P.op("tensor", lambda e, S, rc=rc, j=j, pb=pb: e.matmul(
                            pst[pb][:, 0:128], y_acc[:, j, rc * 128:rc * 128 + 128],
                            idn[:], is_transpose=True),
                            waits=([yacc_rdy] if (rc == 0 and j == 0) else []) +
                                  [("cYS", max(0, P.cnt["cYS"] - 1))],
                            inc="cYT")
                        P.op("vector", lambda e, S, rc=rc, j=j, pb=pb, sl2=sl2: e.tensor_scalar(
                            out=ystage[:, sl2, j * 128:j * 128 + 128],
                            in0=pst[pb][:, 0:128], scalar1=gcol[:, rc:rc + 1],
                            scalar2=None, op0=AL.mult),
                            waits=[P.cur("cYT"), ("cXF", None)] +
                              ([("sAD", max(0, P.cnt["sAD"] - 16))] if j == 0 else []),
                            inc="cYS")
                    P.op("gpsimd", lambda e, S, rc=rc, sl2=sl2: e.indirect_dma_start(
                        out=out, out_offset=bass.IndirectOffsetOnAxis(ap=idtu[:, rc:rc + 1], axis=0),
                        in_=ystage[:, sl2, :], in_offset=None, compute_op=AL.add,
                        bounds_check=S["bndo"], oob_is_err=False),
                        waits=[("cYS", None), ("sAD", None), PH[2]],
                        inc="sAD", n=16)
                    sad_col[rc] = P.cur("sAD")
                ch_free[0] = None
                for rc in range(6):
                    ch_free[rc] = ("cXT", P.cnt["cXT"])

        P.op("gpsimd", lambda e, S: None,
             waits=[("sAD", P.cnt["sAD"])] + PH[1:])
        moe.close()

        with nc.Block() as blk:
            for eng_name in ENGINES:
                dec = getattr(blk, eng_name)

                def make(eng_name=eng_name):
                    def body(e):
                        for en, f in P.steps:
                            if en == eng_name:
                                f(e, S)
                    return body
                dec(make())
    return nc


def host_prep(inputs):
    Wq = np.asarray(inputs["L0_attn_Wq"]); bq = np.asarray(inputs["L0_attn_bq"])
    Wk = np.asarray(inputs["L0_attn_Wk"]); bk = np.asarray(inputs["L0_attn_bk"])
    Wv = np.asarray(inputs["L0_attn_Wv"]); bv = np.asarray(inputs["L0_attn_bv"])
    Wo = np.asarray(inputs["L0_attn_Wo"]); bo = np.asarray(inputs["L0_attn_bo"])
    gw = np.asarray(inputs["L0_ffn_gate_w"]); gb = np.asarray(inputs["L0_ffn_gate_b"])
    w1 = np.asarray(inputs["L0_ffn_w1"]); b1 = np.asarray(inputs["L0_ffn_b1"])
    w2 = np.asarray(inputs["L0_ffn_w2"]); b2 = np.asarray(inputs["L0_ffn_b2"])
    x = np.asarray(inputs["L0_attn_x"])

    p = np.arange(128)
    permq = np.zeros(1024, np.int64)
    for c in range(4):
        h = 4 * c + p // 32
        i = p % 32
        permq[128 * c + p] = 64 * h + 2 * i
        permq[128 * (4 + c) + p] = 64 * h + 2 * i + 1
    g_ = p // 32
    i_ = p % 32
    permk = np.concatenate([64 * g_ + 2 * i_, 64 * g_ + 2 * i_ + 1])
    permo = np.zeros(1024, np.int64)
    for c in range(4):
        h = 4 * c + p // 32
        d_ = p % 32
        permo[128 * c + p] = 64 * h + d_
        permo[128 * (4 + c) + p] = 64 * h + 32 + d_

    Wqp = np.ascontiguousarray(Wq[:, permq]); bqp = bq[permq]
    Wkp = np.ascontiguousarray(Wk[:, permk]); bkp = bk[permk]
    Wop = np.ascontiguousarray(Wo[permo, :])

    bq_sb = np.ascontiguousarray(bqp.reshape(8, 128).T).astype(np.float32)
    bkv = np.stack([bkp[:128], bkp[128:], bv[:128], bv[128:]], axis=1).astype(np.float32)
    bo_sb = np.ascontiguousarray(bo.reshape(8, 128).T).astype(np.float32)
    gw_sb = np.ascontiguousarray(gw.T.reshape(8, 128, 8).transpose(1, 0, 2)).astype(np.float32)
    gb_sb = np.ascontiguousarray(gb.reshape(8, 1)).astype(np.float32)
    b1_sb = np.ascontiguousarray(b1.reshape(8, 32, 128).transpose(2, 0, 1)).astype(np.float32)
    b2_sb = np.ascontiguousarray(b2.reshape(8, 8, 128).transpose(2, 0, 1)).astype(np.float32)

    pos = np.arange(T, dtype=np.float64)
    freq = np.exp(-(2.0 * np.arange(32, dtype=np.float64) / 64.0) * np.log(10000.0))
    fr = np.tile(freq, 4)
    ang = fr[:, None] * pos[None, :]
    cosd = np.cos(ang).astype(np.float32)
    sind = np.sin(ang).astype(np.float32)
    import ml_dtypes
    mskd = np.zeros((128, 4, 512), np.float32)
    ar = np.arange(512)
    for j in range(4):
        for pp in range(128):
            mskd[pp, j, :] = np.where(pp + 128 * j <= ar, 0.0, -1e9)
    mskd = mskd.astype(ml_dtypes.bfloat16)
    idnd = np.eye(128, dtype=np.float32)
    selmd = np.zeros((128, 128), np.float32)
    for pp in range(128):
        selmd[32 * (pp // 32), pp] = 1.0

    tkdd = (np.arange(32)[None, :] * 128 + np.arange(128)[:, None]).astype(np.float32)
    shared = dict(wq=Wqp, wk=Wkp, wv=np.ascontiguousarray(Wv), wo=Wop, tkdd=tkdd,
                  bqd=bq_sb, bkvd=np.ascontiguousarray(bkv), bod=bo_sb,
                  gwd=gw_sb, gbd=gb_sb, b1d=b1_sb, b2d=b2_sb, w1d=w1, w2d=w2,
                  cosd=cosd, sind=sind, mskd=mskd, idnd=idnd, selmd=selmd)
    xs = [np.ascontiguousarray(x[2 * c:2 * c + 2].reshape(NSEQ * T, D).T)
          for c in range(NCORES)]
    return shared, xs


def kernel(**inputs):
    shared, xs = host_prep(inputs)
    nc = build_nc()
    in_maps = [dict(shared, xin=xs[c]) for c in range(NCORES)]
    res = run_bass_kernel_spmd(nc, in_maps, core_ids=list(range(NCORES)))
    outs = [np.asarray(res.results[c]["out"]).reshape(NSEQ, T, D)
            for c in range(NCORES)]
    return np.concatenate(outs, axis=0)
